# version 01
# baseline (speedup 1.0000x reference)
import math
import numpy as np
import ml_dtypes
import concourse.bass as bass
import concourse.mybir as mybir
from concourse.bass_utils import run_bass_kernel_spmd

F32 = mybir.dt.float32
BF16 = mybir.dt.bfloat16
AF = mybir.ActivationFunctionType
ALU = mybir.AluOpType
AX = mybir.AxisListType.X
BF = ml_dtypes.bfloat16

B = 2
T = 4096
D = 2048
NCH = 16
TOK = 1024
NJ = 8
HQ = 16
HKV = 4
DH = 128
HID = 5632
NHC = 44
LAT = 512
DHR = 64
SCL = 1.0 / math.sqrt(128.0)
NEG = -1.0e9
EPS_LN = 1e-5
GN = float(HID * D)

STAGE = 4
DUMP = []
TAPS = []

ENGS = ("tensor", "vector", "scalar", "gpsimd", "sync")


class Rec:
    def __init__(self):
        self.q = {e: [] for e in ENGS}
        self.cnt = {}
        self.handles = {}

    def wait(self, eng, name, val=None):
        v = self.cnt.get(name, 0) if val is None else val
        if v <= 0:
            return
        self.q[eng].append(lambda e, n=name, v=v: e.wait_ge(self.handles[n], v))

    def do(self, eng, fn, inc=None, by=1):
        if inc is not None:
            self.cnt[inc] = self.cnt.get(inc, 0) + by
        def f(e, fn=fn, inc=inc, by=by):
            inst = fn(e)
            if inc is not None:
                inst.then_inc(self.handles[inc], by)
        self.q[eng].append(f)

    def val(self, name):
        return self.cnt.get(name, 0)


def build(nc, stage):
    P = Rec()
    di = {}

    def din(name, shape, dt):
        di[name] = nc.dram_tensor(name, shape, dt, kind="ExternalInput").ap()
        return di[name]

    x_in = din("x_fm", [D, TOK], F32)
    wqkvT = din("wqkvT", [D, 3072], F32)
    woT = din("woT", [D, D], F32)
    din("qn_g", [128, 1], F32); din("qn_b", [128, 1], F32)
    din("kn_g", [128, 1], F32); din("kn_b", [128, 1], F32)
    wdown = din("wdown", [D, 2 * LAT], F32)
    din("wdown_b", [128, 8], F32)
    wukv = din("wukv", [LAT, 2 * D], F32)
    din("wukv_bk", [128, 16], F32)
    wukv_bv = din("wukv_bv", [1, D], F32)
    wuq = din("wuq", [LAT, D], F32)
    din("wuq_b", [128, 16], F32)
    wkr = din("wkr", [D, DHR], F32)
    din("wkr_b", [128, 1], F32)
    wqr = din("wqr", [D, 1024], F32)
    din("wqr_b", [128, 8], F32)
    wout = din("wout", [D, D], F32)
    din("wout_b", [128, 16], F32)
    din("w1s", [256, HID], F32)
    din("w2s", [256, HID], F32)
    din("w3s", [704, D], F32)
    din("cosA", [128, TOK], F32)
    din("sinA", [128, TOK], F32)
    din("cosKR", [128, TOK], F32)
    din("sinKR", [128, TOK], F32)
    cosQR_d = din("cosQR", [128, 8 * TOK], F32)
    sinQR_d = din("sinQR", [128, 8 * TOK], F32)
    din("maskD", [128, 512], F32)
    din("ones_bf", [128, 128], BF16)
    din("ones_f", [128, 128], F32)
    din("iden_bf", [128, 128], BF16)
    din("rotA", [128, 128], BF16)
    din("rotB", [128, 128], BF16)

    out_d = nc.dram_tensor("out", [D, TOK], F32, kind="ExternalOutput").ap()

    kA_loc = nc.dram_tensor("kA_loc", [HKV * DH, TOK], BF16)
    vA_loc = nc.dram_tensor("vA_loc", [TOK, HKV * DH], BF16)
    kA_g = nc.dram_tensor("kA_g", [4 * HKV * DH, TOK], BF16)
    vA_g = nc.dram_tensor("vA_g", [4 * TOK, HKV * DH], BF16)
    kB_loc = nc.dram_tensor("kB_loc", [D + DHR, TOK], BF16)
    vB_loc = nc.dram_tensor("vB_loc", [TOK, D], BF16)
    kB_g = nc.dram_tensor("kB_g", [4 * (D + DHR), TOK], BF16)
    vB_g = nc.dram_tensor("vB_g", [4 * TOK, D], BF16)
    q_sp = nc.dram_tensor("q_sp", [D, TOK], BF16)
    qR_sp = nc.dram_tensor("qR_sp", [1024, TOK], BF16)
    wq1s_d = nc.dram_tensor("wq1s_d", [256, HID], BF16)
    wq2s_d = nc.dram_tensor("wq2s_d", [256, HID], BF16)
    wq3s_d = nc.dram_tensor("wq3s_d", [704, D], BF16)
    wq1_d = nc.dram_tensor("wq1_d", [D, HID], BF16)
    wq2_d = nc.dram_tensor("wq2_d", [D, HID], BF16)
    wq3_d = nc.dram_tensor("wq3_d", [HID, D], BF16)
    gpart_d = nc.dram_tensor("gpart", [1, 16], F32)
    gsum_d = nc.dram_tensor("gsum", [1, 16], F32)
    dbgq_d = nc.dram_tensor("dbgq", [1, 16], F32)

    x_res = nc.alloc_sbuf_tensor("x_res", [128, NCH * TOK], F32)
    xn = nc.alloc_sbuf_tensor("xn", [128, NCH * TOK], BF16)
    kst = nc.alloc_sbuf_tensor("kst", [128, 2 * 4096], BF16)
    vst = nc.alloc_sbuf_tensor("vst", [128, 2 * 4096], BF16)
    pbuf = nc.alloc_sbuf_tensor("pbuf", [128, 2 * 4096], BF16)
    ptb = nc.alloc_sbuf_tensor("ptb", [128, 2 * 1024], BF16)
    kRb = nc.alloc_sbuf_tensor("kRb", [128, 4096], BF16)
    qrt = nc.alloc_sbuf_tensor("qrt", [128, 4 * 1024], BF16)
    wsb = nc.alloc_sbuf_tensor("wsb", [128, 2 * 2048], BF16)
    qwork = nc.alloc_sbuf_tensor("qwork", [128, 2 * 1024], BF16)
    scr = nc.alloc_sbuf_tensor("scr", [128, 1024], F32)
    scr2 = nc.alloc_sbuf_tensor("scr2", [128, 1024], F32)
    sqs = nc.alloc_sbuf_tensor("sqs", [128, 1024], BF16)
    maskb = nc.alloc_sbuf_tensor("maskb", [128, 512], F32)
    consts = nc.alloc_sbuf_tensor("consts", [128, 4 * 128], BF16)
    constf = nc.alloc_sbuf_tensor("constf", [128, 128], F32)
    small = nc.alloc_sbuf_tensor("small", [128, 96], F32)
    lrow = nc.alloc_sbuf_tensor("lrow", [128, 16], F32)

    pbf = pbuf[:].bitcast(F32)
    rows = lambda off, n: pbf[0:1, off:off + n]
    tbl = lambda off, n: pbf[:, 2048 + off: 2048 + off + n]

    SL_QG, SL_QB, SL_KG, SL_KB = 0, 1, 2, 3
    SL_DWB0 = 8
    SL_UKB0 = 16
    SL_UQB0 = 32
    SL_KRB = 48
    SL_QRB0 = 49
    SL_OB0 = 57
    SL_G1, SL_G2, SL_G3 = 80, 81, 82
    nc.const_aps.register(F32, EPS_LN, small[:, 90:91])

    psA = nc.alloc_psum_tensor("psA", [128, 1024], F32)
    psB = nc.alloc_psum_tensor("psB", [128, 1024], F32)
    psT0 = nc.alloc_psum_tensor("psT0", [128, 1024], BF16)
    psT1 = nc.alloc_psum_tensor("psT1", [128, 1024], BF16)
    psE = nc.alloc_psum_tensor("psE", [128, 512], F32)
    psF = nc.alloc_psum_tensor("psF", [128, 512], F32)

    onesb1 = lambda: consts[:, 0:1]
    iden = lambda: consts[:, 128:256]
    rotAc = lambda: consts[:, 256:384]
    rotBc = lambda: consts[:, 384:512]
    onesf_r = lambda: constf[0:1, 0:128]

    dI, dS, dG, cc = "dI", "dS", "dG", "cc"
    tS, vS, aS, pS = "tS", "vS", "aS", "pS"
    dP = ("dP0", "dP1")
    dQ = ("dQ0", "dQ1")

    def loads():
        P.do("sync", lambda e: e.dma_start(out=x_res[:].rearrange("p (i t) -> p i t", t=1024),
                                           in_=x_in.rearrange("(i p) t -> p i t", p=128)), inc=dI, by=16)
        P.do("sync", lambda e: e.dma_start(out=consts[:, 0:128], in_=di["ones_bf"]), inc=dI, by=16)
        P.do("sync", lambda e: e.dma_start(out=consts[:, 128:256], in_=di["iden_bf"]), inc=dI, by=16)
        P.do("sync", lambda e: e.dma_start(out=consts[:, 256:384], in_=di["rotA"]), inc=dI, by=16)
        P.do("sync", lambda e: e.dma_start(out=consts[:, 384:512], in_=di["rotB"]), inc=dI, by=16)
        P.do("sync", lambda e: e.dma_start(out=constf[:], in_=di["ones_f"]), inc=dI, by=16)
        P.do("sync", lambda e: e.dma_start(out=maskb[:], in_=di["maskD"]), inc=dI, by=16)
        for nm, col, n in [("qn_g", SL_QG, 1), ("qn_b", SL_QB, 1), ("kn_g", SL_KG, 1), ("kn_b", SL_KB, 1),
                           ("wdown_b", SL_DWB0, 8), ("wukv_bk", SL_UKB0, 16), ("wuq_b", SL_UQB0, 16),
                           ("wkr_b", SL_KRB, 1), ("wqr_b", SL_QRB0, 8), ("wout_b", SL_OB0, 16)]:
            P.do("sync", lambda e, nm=nm, col=col, n=n: e.dma_start(out=small[:, col:col + n], in_=di[nm]), inc=dI, by=16)
        ld = P.val(dI)
        P.wait("vector", dI, ld)
        P.do("vector", lambda e: e.memset(small[:, 90:91], EPS_LN))
        P.do("vector", lambda e: e.tensor_single_scalar(small[:, SL_QG:SL_QG + 1], small[:, SL_QG:SL_QG + 1], SCL, ALU.mult))
        P.do("vector", lambda e: e.tensor_single_scalar(small[:, SL_QB:SL_QB + 1], small[:, SL_QB:SL_QB + 1], SCL, ALU.mult), inc=vS)
        return ld

    def load_tables(cos_ap, sin_ap):
        P.wait("sync", tS)
        P.wait("sync", aS)
        P.do("sync", lambda e: e.dma_start(out=tbl(0, 1024), in_=cos_ap), inc=dI, by=16)
        P.do("sync", lambda e: e.dma_start(out=tbl(1024, 1024), in_=sin_ap), inc=dI, by=16)
        return P.val(dI)

    def layernorm():
        P.wait("gpsimd", dI)
        P.wait("gpsimd", vS)
        for th in range(2):
            o = th * 512
            for i in range(NCH):
                P.wait("gpsimd", tS)
                P.do("gpsimd", lambda e, i=i, o=o: e.tensor_copy(sqs[:, 0:512], x_res[:, i * 1024 + o: i * 1024 + o + 512]), inc=pS)
                P.wait("scalar", pS)
                P.do("scalar", lambda e: e.activation(ptb[:, 0:512], sqs[:, 0:512], AF.Square), inc=aS)
                P.wait("tensor", pS)
                first, last = (i == 0), (i == NCH - 1)
                if first:
                    P.wait("tensor", vS)
                P.do("tensor", lambda e, first=first, last=last: e.matmul(psA[0:1, 0:512], onesb1(), sqs[:, 0:512], start=first, stop=last))
                P.wait("tensor", aS)
                P.do("tensor", lambda e, first=first, last=last: e.matmul(psA[0:1, 512:1024], onesb1(), ptb[:, 0:512], start=first, stop=last), inc=tS)
            P.wait("vector", tS)
            P.do("vector", lambda e: e.tensor_single_scalar(rows(0, 512), psA[0:1, 0:512], 1.0 / D, ALU.mult))
            P.do("vector", lambda e: e.tensor_single_scalar(rows(1536, 512), psA[0:1, 512:1024], 1.0 / D, ALU.mult), inc=vS)
            P.wait("scalar", vS)
            P.do("scalar", lambda e: e.activation(rows(512, 512), rows(0, 512), AF.Square), inc=aS)
            P.wait("vector", aS)
            P.do("vector", lambda e: e.tensor_sub(rows(512, 512), rows(1536, 512), rows(512, 512)), inc=vS)
            P.wait("scalar", vS)
            P.do("scalar", lambda e: e.activation(rows(512, 512), rows(512, 512), AF.Sqrt, bias=EPS_LN), inc=aS)
            P.wait("vector", aS)
            P.do("vector", lambda e: e.reciprocal(rows(512, 512), rows(512, 512)), inc=vS)
            P.wait("vector", vS)
            P.do("vector", lambda e: e.scalar_tensor_tensor(rows(1024, 512), rows(0, 512), -1.0, rows(512, 512), ALU.mult, ALU.mult), inc=vS)
            P.wait("tensor", vS)
            P.do("tensor", lambda e: e.matmul(psA[:, 0:512], onesf_r(), rows(512, 512), start=True, stop=True))
            P.do("tensor", lambda e: e.matmul(psA[:, 512:1024], onesf_r(), rows(1024, 512), start=True, stop=True), inc=tS)
            P.wait("vector", tS)
            for i in range(NCH):
                P.do("vector", lambda e, i=i, o=o: e.tensor_mul(scr[:, 0:512], x_res[:, i * 1024 + o: i * 1024 + o + 512], psA[:, 0:512]), inc=vS)
                P.wait("vector", vS)
                P.do("vector", lambda e, i=i, o=o: e.tensor_add(xn[:, i * 1024 + o: i * 1024 + o + 512], scr[:, 0:512], psA[:, 512:1024]), inc=vS)
        P.wait("vector", vS)
        return P.val(vS)

    class Stream:

        def __init__(self, queue, sems):
            self.queue = queue
            self.sems = sems
            self.idx = 0
            self.marks = []
            self.entry = P.val(tS)

        def load(self, fn):
            buf = self.idx % 2
            if self.idx >= 2:
                P.wait(self.queue, tS, self.marks[self.idx - 2])
            else:
                P.wait(self.queue, tS, self.entry)
            sem = self.sems[buf]
            P.do(self.queue, lambda e, fn=fn, buf=buf: fn(e, buf), inc=sem, by=16)
            return buf, sem, P.val(sem)

        def mark(self):
            self.marks.append(P.val(tS))
            self.idx += 1

    def wsb_block(w_ap, oc, nin=NCH):
        def fn(e, buf):
            return e.dma_start(
                out=wsb[:, buf * 2048: buf * 2048 + nin * 128].rearrange("p (i c) -> p i c", c=128),
                in_=w_ap[:, oc * 128:(oc + 1) * 128].rearrange("(i p) c -> p i c", p=128))
        return fn

    def proj_fm(w_ap, out_chunks, evac, nin=NCH, extra_wait=None):
        st = Stream("gpsimd", dQ)
        for bi, oc in enumerate(out_chunks):
            buf, sem, rdy = st.load(wsb_block(w_ap, oc, nin))
            P.wait("tensor", sem, rdy)
            if extra_wait is not None:
                P.wait("tensor", extra_wait[0], extra_wait[1])
            P.wait("tensor", vS)
            ps = psA if bi % 2 == 0 else psB
            for tt in range(2):
                for i in range(nin):
                    P.do("tensor", lambda e, buf=buf, i=i, tt=tt, ps=ps, nin=nin: e.matmul(
                        ps[:, tt * 512:(tt + 1) * 512],
                        wsb[:, buf * 2048 + i * 128: buf * 2048 + (i + 1) * 128],
                        xn[:, i * 1024 + tt * 512: i * 1024 + (tt + 1) * 512],
                        start=(i == 0), stop=(i == nin - 1)), inc=(tS if i == nin - 1 else None))
            st.mark()
            evac(oc, ps)

    def head_stats(src):
        P.wait("scalar", vS)
        P.do("scalar", lambda e, src=src: e.activation(sqs[:, 0:1024], src, AF.Square), inc=aS)
        P.wait("tensor", aS)
        P.wait("tensor", vS)
        P.do("tensor", lambda e, src=src: e.matmul(psE[0:1, 0:512], onesb1(), src[:, 0:512], start=True, stop=True))
        P.do("tensor", lambda e, src=src: e.matmul(psF[0:1, 0:512], onesb1(), src[:, 512:1024], start=True, stop=True))
        P.do("tensor", lambda e: e.matmul(psA[0:1, 0:512], onesb1(), sqs[:, 0:512], start=True, stop=True))
        P.do("tensor", lambda e: e.matmul(psA[0:1, 512:1024], onesb1(), sqs[:, 512:1024], start=True, stop=True), inc=tS)
        P.wait("vector", tS)
        P.do("vector", lambda e: e.tensor_single_scalar(rows(0, 512), psE[0:1, 0:512], 1.0 / 128, ALU.mult))
        P.do("vector", lambda e: e.tensor_single_scalar(rows(512, 512), psF[0:1, 0:512], 1.0 / 128, ALU.mult))
        P.do("vector", lambda e: e.tensor_single_scalar(rows(1024, 1024), psA[0:1, 0:1024], 1.0 / 128, ALU.mult), inc=vS)
        P.wait("scalar", vS)
        P.do("scalar", lambda e: e.activation(scr2[0:1, 0:1024], rows(0, 1024), AF.Square), inc=aS)
        P.wait("vector", aS)
        P.do("vector", lambda e: e.tensor_sub(rows(1024, 1024), rows(1024, 1024), scr2[0:1, 0:1024]), inc=vS)
        P.wait("scalar", vS)
        P.do("scalar", lambda e: e.activation(rows(1024, 1024), rows(1024, 1024), AF.Sqrt, bias=EPS_LN), inc=aS)
        P.wait("vector", aS)
        P.do("vector", lambda e: e.reciprocal(rows(1024, 1024), rows(1024, 1024)), inc=vS)
        P.wait("vector", vS)
        P.do("vector", lambda e: e.scalar_tensor_tensor(rows(0, 1024), rows(0, 1024), -1.0, rows(1024, 1024), ALU.mult, ALU.mult), inc=vS)
        P.wait("tensor", vS)
        for half in range(2):
            P.do("tensor", lambda e, half=half: e.matmul(psA[:, half * 512:(half + 1) * 512], onesf_r(), rows(1024 + half * 512, 512), start=True, stop=True))
            P.do("tensor", lambda e, half=half: e.matmul(psB[:, half * 512:(half + 1) * 512], onesf_r(), rows(half * 512, 512), start=True, stop=True),
                 inc=(tS if half == 1 else None))

    def rope_apply(use_rotA, par_out, nrows=128):
        nr = nrows
        P.wait("gpsimd", vS)
        P.do("gpsimd", lambda e, nr=nr: e.tensor_copy(sqs[0:nr, 0:1024], scr[0:nr, 0:1024]), inc=pS)
        P.wait("tensor", pS)
        P.wait("tensor", vS)
        rot = rotAc if use_rotA else rotBc
        P.do("tensor", lambda e, rot=rot, nr=nr: e.matmul(psA[0:nr, 0:512], rot()[0:nr, 0:nr], sqs[0:nr, 0:512], start=True, stop=True))
        P.do("tensor", lambda e, rot=rot, nr=nr: e.matmul(psA[0:nr, 512:1024], rot()[0:nr, 0:nr], sqs[0:nr, 512:1024], start=True, stop=True), inc=tS)
        P.wait("vector", tS)
        P.do("vector", lambda e, nr=nr: e.tensor_mul(scr2[0:nr, 0:1024], psA[0:nr, 0:1024], tbl(1024, 1024)[0:nr, :]), inc=vS)
        P.do("vector", lambda e, nr=nr: e.tensor_mul(scr[0:nr, 0:1024], scr[0:nr, 0:1024], tbl(0, 1024)[0:nr, :]), inc=vS)
        P.wait("vector", vS)
        P.do("vector", lambda e, par=par_out, nr=nr: e.tensor_add(qwork[0:nr, par * 1024:(par + 1) * 1024], scr[0:nr, 0:1024], scr2[0:nr, 0:1024]), inc=vS)

    def attn_a_prep(ln_done):
        tb_done = load_tables(di["cosA"], di["sinA"])
        P.wait("vector", dI, tb_done)
        st = Stream("gpsimd", dQ)
        for idx in range(20):
            is_q = idx < 16
            oc = idx
            par = idx % 2
            buf, sem, rdy = st.load(wsb_block(wqkvT, oc))
            P.wait("tensor", sem, rdy)
            P.wait("tensor", vS, ln_done)
            P.wait("tensor", vS)
            for tt in range(2):
                ps = psE if tt == 0 else psF
                for i in range(NCH):
                    P.do("tensor", lambda e, buf=buf, i=i, ps=ps, tt=tt: e.matmul(
                        ps[:, 0:512],
                        wsb[:, buf * 2048 + i * 128: buf * 2048 + (i + 1) * 128],
                        xn[:, i * 1024 + tt * 512: i * 1024 + (tt + 1) * 512],
                        start=(i == 0), stop=(i == NCH - 1)), inc=(tS if i == NCH - 1 else None))
            st.mark()
            P.wait("vector", tS)
            P.wait("vector", dS)
            P.do("vector", lambda e, par=par: e.tensor_copy(qwork[:, par * 1024: par * 1024 + 512], psE[:, 0:512]))
            P.do("vector", lambda e, par=par: e.tensor_copy(qwork[:, par * 1024 + 512: par * 1024 + 1024], psF[:, 0:512]), inc=vS)
            src = qwork[:, par * 1024:(par + 1) * 1024]
            head_stats(src)
            g_slot = SL_QG if is_q else SL_KG
            b_slot = SL_QB if is_q else SL_KB
            P.wait("vector", tS)
            P.do("vector", lambda e, src=src: e.tensor_mul(scr[:, 0:1024], src, psA[:, 0:1024]), inc=vS)
            P.wait("vector", vS)
            P.do("vector", lambda e: e.tensor_add(scr[:, 0:1024], scr[:, 0:1024], psB[:, 0:1024]), inc=vS)
            P.wait("vector", vS)
            P.do("vector", lambda e, g=g_slot, b=b_slot: e.tensor_scalar(
                scr[:, 0:1024], scr[:, 0:1024], small[:, g:g + 1], small[:, b:b + 1], ALU.mult, ALU.add), inc=vS)
            rope_apply(True, par)
            P.wait("sync", vS)
            if is_q:
                P.do("sync", lambda e, oc=oc, par=par: e.dma_start(
                    out=q_sp.ap()[oc * 128:(oc + 1) * 128, :], in_=qwork[:, par * 1024:(par + 1) * 1024]), inc=dS, by=16)
            else:
                P.do("sync", lambda e, oc=oc, par=par: e.dma_start(
                    out=kA_loc.ap()[(oc - 16) * 128:(oc - 15) * 128, :], in_=qwork[:, par * 1024:(par + 1) * 1024]), inc=dS, by=16)
        for vb in range(4):
            def vload(e, buf, vb=vb):
                return e.dma_start(
                    out=wsb[:, buf * 2048: buf * 2048 + 2048].rearrange("p (i c) -> p i c", c=128),
                    in_=wqkvT[:, 2560 + vb * 128: 2560 + (vb + 1) * 128].rearrange("(i p) c -> p i c", p=128))
            buf, sem, rdy = st.load(vload)
            P.wait("tensor", sem, rdy)
            for ts in range(NJ):
                ps = psE if ts % 2 == 0 else psF
                P.wait("tensor", vS)
                for i in range(NCH):
                    P.do("tensor", lambda e, ts=ts, i=i, ps=ps, buf=buf: e.matmul(
                        ps[:, 0:128],
                        xn[:, i * 1024 + ts * 128: i * 1024 + (ts + 1) * 128],
                        wsb[:, buf * 2048 + i * 128: buf * 2048 + (i + 1) * 128],
                        start=(i == 0), stop=(i == NCH - 1)), inc=(tS if i == NCH - 1 else None))
                P.wait("vector", tS)
                P.do("vector", lambda e, ts=ts, vb=vb, ps=ps: e.tensor_copy(
                    vst[:, ts * 512 + vb * 128: ts * 512 + (vb + 1) * 128], ps[:, 0:128]), inc=vS)
            st.mark()
        P.wait("sync", vS)
        P.do("sync", lambda e: e.dma_start(
            out=vA_loc.ap().rearrange("(ts p) c -> p ts c", p=128),
            in_=vst[:, 0:4096].rearrange("p (ts c) -> p ts c", c=512)), inc=dS, by=16)
        sp_done = P.val(dS)
        P.wait("gpsimd", dS, sp_done)
        P.do("gpsimd", lambda e: e.collective_compute(
            "AllGather", ALU.bypass, replica_groups=[[0, 1, 2, 3], [4, 5, 6, 7]],
            ins=[kA_loc.ap().opt()], outs=[kA_g.ap().opt()]), inc=cc)
        P.do("gpsimd", lambda e: e.collective_compute(
            "AllGather", ALU.bypass, replica_groups=[[0, 1, 2, 3], [4, 5, 6, 7]],
            ins=[vA_loc.ap().opt()], outs=[vA_g.ap().opt()]), inc=cc)
        return P.val(cc), sp_done

    def attn_b_prep(ln_done):
        def evac_lat(oc, ps):
            P.wait("vector", tS)
            for tt in range(2):
                P.do("vector", lambda e, oc=oc, tt=tt, ps=ps: e.tensor_scalar(
                    kst[:, oc * 1024 + tt * 512: oc * 1024 + (tt + 1) * 512],
                    ps[:, tt * 512:(tt + 1) * 512], small[:, SL_DWB0 + oc:SL_DWB0 + oc + 1], None, ALU.add), inc=vS)
        proj_fm(wdown, list(range(8)), evac_lat, extra_wait=(vS, ln_done))
        lat_done = P.val(vS)
        cKV = kst[:, 0:4096]
        cQ = kst[:, 4096:8192]

        tb2 = load_tables(di["cosKR"], di["sinKR"])
        P.wait("vector", dI, tb2)
        st = Stream("gpsimd", dQ)
        def krload(e, buf):
            return e.dma_start(
                out=wsb[:, buf * 2048: buf * 2048 + NCH * 64].rearrange("p (i c) -> p i c", c=64),
                in_=wkr.rearrange("(i p) c -> p i c", p=128))
        buf, sem, rdy = st.load(krload)
        P.wait("tensor", sem, rdy)
        P.wait("tensor", vS)
        for tt in range(2):
            ps = psE if tt == 0 else psF
            for i in range(NCH):
                P.do("tensor", lambda e, buf=buf, i=i, ps=ps, tt=tt: e.matmul(
                    ps[0:64, 0:512],
                    wsb[:, buf * 2048 + i * 64: buf * 2048 + (i + 1) * 64],
                    xn[:, i * 1024 + tt * 512: i * 1024 + (tt + 1) * 512],
                    start=(i == 0), stop=(i == NCH - 1)), inc=(tS if i == NCH - 1 else None))
        st.mark()
        P.wait("vector", tS)
        for tt in range(2):
            ps = psE if tt == 0 else psF
            P.do("vector", lambda e, tt=tt, ps=ps: e.tensor_scalar(
                scr[0:64, tt * 512:(tt + 1) * 512], ps[0:64, 0:512], small[0:64, SL_KRB:SL_KRB + 1], None, ALU.add), inc=vS)
        rope_apply(False, 0, nrows=64)
        P.wait("sync", vS)
        P.do("sync", lambda e: e.dma_start(out=kB_loc.ap()[D:D + DHR, :], in_=qwork[0:64, 0:1024]), inc=dS, by=16)

        for ch in range(8):
            tbq = load_tables(cosQR_d[:, ch * 1024:(ch + 1) * 1024], sinQR_d[:, ch * 1024:(ch + 1) * 1024])
            P.wait("vector", dI, tbq)
            buf, sem, rdy = st.load(wsb_block(wqr, ch))
            P.wait("tensor", sem, rdy)
            P.wait("tensor", vS)
            for tt in range(2):
                ps = psE if tt == 0 else psF
                for i in range(NCH):
                    P.do("tensor", lambda e, buf=buf, i=i, ps=ps, tt=tt: e.matmul(
                        ps[:, 0:512],
                        wsb[:, buf * 2048 + i * 128: buf * 2048 + (i + 1) * 128],
                        xn[:, i * 1024 + tt * 512: i * 1024 + (tt + 1) * 512],
                        start=(i == 0), stop=(i == NCH - 1)), inc=(tS if i == NCH - 1 else None))
            st.mark()
            P.wait("vector", tS)
            P.wait("vector", dS)
            for tt in range(2):
                ps = psE if tt == 0 else psF
                P.do("vector", lambda e, tt=tt, ps=ps, ch=ch: e.tensor_scalar(
                    scr[:, tt * 512:(tt + 1) * 512], ps[:, 0:512],
                    small[:, SL_QRB0 + ch:SL_QRB0 + ch + 1], SCL, ALU.add, ALU.mult), inc=vS)
            rope_apply(False, 0)
            P.wait("sync", vS)
            P.do("sync", lambda e, ch=ch: e.dma_start(out=qR_sp.ap()[ch * 128:(ch + 1) * 128, :], in_=qwork[:, 0:1024]), inc=dS, by=16)

        for which in ("k", "q"):
            w_ap = wukv if which == "k" else wuq
            src = cKV if which == "k" else cQ
            for oc in range(16):
                def kqload(e, buf, oc=oc, w_ap=w_ap):
                    return e.dma_start(
                        out=wsb[:, buf * 2048: buf * 2048 + 4 * 128].rearrange("p (i c) -> p i c", c=128),
                        in_=w_ap[:, oc * 128:(oc + 1) * 128].rearrange("(i p) c -> p i c", p=128))
                buf, sem, rdy = st.load(kqload)
                P.wait("tensor", sem, rdy)
                P.wait("tensor", vS, lat_done)
                P.wait("tensor", vS)
                for tt in range(2):
                    ps = psE if tt == 0 else psF
                    for i in range(4):
                        P.do("tensor", lambda e, buf=buf, i=i, ps=ps, tt=tt, src=src: e.matmul(
                            ps[:, 0:512],
                            wsb[:, buf * 2048 + i * 128: buf * 2048 + (i + 1) * 128],
                            src[:, i * 1024 + tt * 512: i * 1024 + (tt + 1) * 512],
                            start=(i == 0), stop=(i == 3)), inc=(tS if i == 3 else None))
                st.mark()
                P.wait("vector", tS)
                P.wait("vector", dS)
                par = oc % 2
                for tt in range(2):
                    ps = psE if tt == 0 else psF
                    if which == "k":
                        P.do("vector", lambda e, tt=tt, ps=ps, oc=oc, par=par: e.tensor_scalar(
                            qwork[:, par * 1024 + tt * 512: par * 1024 + (tt + 1) * 512],
                            ps[:, 0:512], small[:, SL_UKB0 + oc:SL_UKB0 + oc + 1], None, ALU.add), inc=vS)
                    else:
                        P.do("vector", lambda e, tt=tt, ps=ps, oc=oc, par=par: e.tensor_scalar(
                            qwork[:, par * 1024 + tt * 512: par * 1024 + (tt + 1) * 512],
                            ps[:, 0:512], small[:, SL_UQB0 + oc:SL_UQB0 + oc + 1], SCL, ALU.add, ALU.mult), inc=vS)
                P.wait("sync", vS)
                dst = kB_loc.ap()[oc * 128:(oc + 1) * 128, :] if which == "k" else q_sp.ap()[oc * 128:(oc + 1) * 128, :]
                P.do("sync", lambda e, dst=dst, par=par: e.dma_start(out=dst, in_=qwork[:, par * 1024:(par + 1) * 1024]), inc=dS, by=16)

        P.wait("gpsimd", vS)
        P.do("gpsimd", lambda e: e.dma_start(out=qrt[0:1, 0:2048], in_=wukv_bv), inc=dG, by=16)
        bv_done = P.val(dG)
        for vb in range(16):
            def vload(e, buf, vb=vb):
                return e.dma_start(
                    out=wsb[:, buf * 2048: buf * 2048 + 4 * 128].rearrange("p (i c) -> p i c", c=128),
                    in_=wukv[:, D + vb * 128: D + (vb + 1) * 128].rearrange("(i p) c -> p i c", p=128))
            buf, sem, rdy = st.load(vload)
            P.wait("tensor", sem, rdy)
            P.wait("tensor", dG, bv_done)
            off = (vb % 2) * 1024
            for ts in range(NJ):
                ps = psE if ts % 2 == 0 else psF
                P.wait("tensor", vS)
                for i in range(4):
                    P.do("tensor", lambda e, ts=ts, i=i, ps=ps, buf=buf: e.matmul(
                        ps[:, 0:128],
                        cKV[:, i * 1024 + ts * 128: i * 1024 + (ts + 1) * 128],
                        wsb[:, buf * 2048 + i * 128: buf * 2048 + (i + 1) * 128],
                        start=(i == 0), stop=False))
                P.do("tensor", lambda e, ps=ps, vb=vb: e.matmul(
                    ps[:, 0:128], consts[0:1, 0:128],
                    qrt[0:1, vb * 128:(vb + 1) * 128],
                    start=False, stop=True), inc=tS)
                P.wait("vector", tS)
                if ts == 0:
                    P.wait("vector", dS)
                P.do("vector", lambda e, ts=ts, off=off, ps=ps: e.tensor_copy(
                    vst[:, off + ts * 128: off + (ts + 1) * 128], ps[:, 0:128]), inc=vS)
            st.mark()
            P.wait("sync", vS)
            P.do("sync", lambda e, vb=vb, off=off: e.dma_start(
                out=vB_loc.ap().rearrange("(ts p) c -> p ts c", p=128)[:, :, vb * 128:(vb + 1) * 128],
                in_=vst[:, off:off + 1024].rearrange("p (ts c) -> p ts c", c=128)), inc=dS, by=16)
        sp_done = P.val(dS)
        P.wait("gpsimd", dS, sp_done)
        P.do("gpsimd", lambda e: e.collective_compute(
            "AllGather", ALU.bypass, replica_groups=[[0, 1, 2, 3], [4, 5, 6, 7]],
            ins=[kB_loc.ap().opt()], outs=[kB_g.ap().opt()]), inc=cc)
        P.do("gpsimd", lambda e: e.collective_compute(
            "AllGather", ALU.bypass, replica_groups=[[0, 1, 2, 3], [4, 5, 6, 7]],
            ins=[vB_loc.ap().opt()], outs=[vB_g.ap().opt()]), inc=cc)
        return P.val(cc), sp_done

    def attention(kind, ag_done, sp_done):
        kg = kA_g if kind == "a" else kB_g
        vg = vA_g if kind == "a" else vB_g
        KROWS = HKV * DH if kind == "a" else D + DHR
        exp_seen = {0: P.val(aS), 1: P.val(aS)}
        hmark = {}
        entry = P.val(tS)
        unit = 0
        if kind == "b":
            P.wait("sync", cc, ag_done)
            P.wait("sync", tS, entry)
            for r in range(4):
                P.do("sync", lambda e, r=r: e.dma_start(
                    out=kRb[0:64, :].rearrange("p (g c) -> p g c", c=128)[:, r::4, :],
                    in_=kg.ap()[r * KROWS + D: r * KROWS + D + DHR, :].rearrange("p (g c) -> p g c", c=128)), inc=dI, by=16)
            kr_ready = P.val(dI)
            P.wait("tensor", dI, kr_ready)
        for h in range(16):
            hb = h % 2
            kh = (h // 4) if kind == "a" else h
            if h == 0:
                P.wait("sync", cc, ag_done)
                P.wait("sync", dS, sp_done)
            P.wait("sync", tS, hmark[h - 2] if h >= 2 else entry)
            sem = dP[hb]
            for r in range(4):
                P.do("sync", lambda e, r=r, hb=hb, kh=kh: e.dma_start(
                    out=kst[:, hb * 4096:(hb + 1) * 4096].rearrange("p (g c) -> p g c", c=128)[:, r::4, :],
                    in_=kg.ap()[r * KROWS + kh * 128: r * KROWS + (kh + 1) * 128, :].rearrange("p (g c) -> p g c", c=128)), inc=sem, by=16)
                P.do("sync", lambda e, r=r, hb=hb, kh=kh: e.dma_start(
                    out=vst[:, hb * 4096:(hb + 1) * 4096].rearrange("p (g c) -> p g c", c=128)[:, r::4, :],
                    in_=vg.ap().rearrange("(r g p) c -> r g p c", r=4, p=128)[r, :, :, kh * 128:(kh + 1) * 128].rearrange("g p c -> p g c")), inc=sem, by=16)
            P.do("sync", lambda e, h=h, hb=hb: e.dma_start(
                out=qrt[:, hb * 2048: hb * 2048 + 1024], in_=q_sp.ap()[h * 128:(h + 1) * 128, :]), inc=sem, by=16)
            if kind == "b":
                P.do("sync", lambda e, h=h, hb=hb: e.dma_start(
                    out=qrt[0:64, hb * 2048 + 1024: hb * 2048 + 2048], in_=qR_sp.ap()[h * 64:(h + 1) * 64, :]), inc=sem, by=16)
            kv_ready = P.val(sem)

            for j in range(NJ):
                pref = (j + 1) * 512
                nblk = (pref + 1023) // 1024
                pb_off = (unit % 2) * 4096
                for blk in range(nblk):
                    c0 = blk * 1024
                    bw = min(1024, pref - c0)
                    rsel = (unit + blk) % 2
                    reg = psA if rsel == 0 else psB
                    P.wait("tensor", sem, kv_ready)
                    P.wait("tensor", aS, exp_seen[rsel])
                    nch_here = (bw + 511) // 512
                    for ch in range(nch_here):
                        cw = min(512, bw - ch * 512)
                        lastA = (kind == "a") and (ch == nch_here - 1)
                        P.do("tensor", lambda e, hb=hb, j=j, c0=c0, ch=ch, cw=cw, reg=reg: e.matmul(
                            reg[:, ch * 512: ch * 512 + cw],
                            qrt[:, hb * 2048 + j * 128: hb * 2048 + (j + 1) * 128],
                            kst[:, hb * 4096 + c0 + ch * 512: hb * 4096 + c0 + ch * 512 + cw],
                            start=True, stop=(kind == "a")), inc=(tS if lastA else None))
                    if kind == "b":
                        for ch in range(nch_here):
                            cw = min(512, bw - ch * 512)
                            P.do("tensor", lambda e, j=j, hb=hb, c0=c0, ch=ch, cw=cw, reg=reg: e.matmul(
                                reg[:, ch * 512: ch * 512 + cw],
                                qrt[0:64, hb * 2048 + 1024 + j * 128: hb * 2048 + 1024 + (j + 1) * 128],
                                kRb[0:64, c0 + ch * 512: c0 + ch * 512 + cw],
                                start=False, stop=True), inc=(tS if ch == nch_here - 1 else None))
                    s_done = P.val(tS)
                    if c0 + bw == pref:
                        off = (pref - 512) - c0
                        P.wait("vector", tS, s_done)
                        P.do("vector", lambda e, reg=reg, off=off: e.tensor_add(reg[:, off:off + 512], reg[:, off:off + 512], maskb[:]), inc=vS)
                    P.wait("vector", tS, s_done)
                    P.do("vector", lambda e, reg=reg, bw=bw: e.tensor_reduce(lrow[:, 2:3], reg[:, 0:bw], AX, ALU.max, negate=True), inc=vS)
                    if blk == 0:
                        P.wait("vector", vS)
                        P.do("vector", lambda e: e.tensor_copy(lrow[:, 0:1], lrow[:, 2:3]), inc=vS)
                        P.wait("scalar", vS)
                        P.do("scalar", lambda e, reg=reg, bw=bw, pb=pb_off, c0=c0: e.activation(
                            pbuf[:, pb + c0: pb + c0 + bw], reg[:, 0:bw], AF.Exp,
                            bias=lrow[:, 0:1], accum_out=lrow[:, 1:2]), inc=aS)
                    else:
                        P.wait("vector", vS)
                        P.do("vector", lambda e: e.tensor_copy(lrow[:, 3:4], lrow[:, 0:1]))
                        P.do("vector", lambda e: e.tensor_tensor(lrow[:, 0:1], lrow[:, 0:1], lrow[:, 2:3], ALU.min), inc=vS)
                        P.wait("vector", vS)
                        P.do("vector", lambda e: e.tensor_sub(lrow[:, 3:4], lrow[:, 0:1], lrow[:, 3:4]), inc=vS)
                        P.wait("scalar", vS)
                        P.do("scalar", lambda e: e.activation(lrow[:, 4:5], lrow[:, 3:4], AF.Exp), inc=aS)
                        P.wait("vector", aS)
                        P.do("vector", lambda e, pb=pb_off, c0=c0: e.tensor_scalar(
                            pbuf[:, pb:pb + c0], pbuf[:, pb:pb + c0], lrow[:, 4:5], None, ALU.mult))
                        P.do("vector", lambda e: e.tensor_mul(lrow[:, 1:2], lrow[:, 1:2], lrow[:, 4:5]), inc=vS)
                        P.wait("scalar", vS)
                        P.do("scalar", lambda e, reg=reg, bw=bw, pb=pb_off, c0=c0: e.activation(
                            pbuf[:, pb + c0: pb + c0 + bw], reg[:, 0:bw], AF.Exp,
                            bias=lrow[:, 0:1], accum_out=lrow[:, 5:6]), inc=aS)
                        P.wait("vector", aS)
                        P.do("vector", lambda e: e.tensor_add(lrow[:, 1:2], lrow[:, 1:2], lrow[:, 5:6]), inc=vS)
                    exp_seen[rsel] = P.val(aS)
                exp_a = P.val(aS)
                P.wait("vector", aS, exp_a)
                P.wait("vector", vS)
                P.do("vector", lambda e: e.reciprocal(lrow[:, 6:7], lrow[:, 1:2]), inc=vS)
                recip_done = P.val(vS)
                nkt = pref // 128
                pvps = psE if unit % 2 == 0 else psF
                for tb in range((nkt + 7) // 8):
                    tps = psT0 if tb % 2 == 0 else psT1
                    to = (tb % 2) * 1024
                    n_in = min(8, nkt - tb * 8)
                    P.wait("tensor", aS, exp_a)
                    P.wait("tensor", vS)
                    for t in range(n_in):
                        kt = tb * 8 + t
                        P.do("tensor", lambda e, kt=kt, t=t, tps=tps, pb=pb_off: e.transpose(
                            tps[:, t * 128:(t + 1) * 128], pbuf[:, pb + kt * 128: pb + (kt + 1) * 128], iden()),
                            inc=(tS if t == n_in - 1 else None))
                    P.wait("vector", tS)
                    P.do("vector", lambda e, tps=tps, to=to, n=n_in: e.tensor_copy(ptb[:, to:to + n * 128], tps[:, 0:n * 128]), inc=vS)
                    P.wait("tensor", vS)
                    for t in range(n_in):
                        kt = tb * 8 + t
                        P.do("tensor", lambda e, kt=kt, t=t, to=to, hb=hb, pvps=pvps, nkt=nkt: e.matmul(
                            pvps[:, 0:128], ptb[:, to + t * 128: to + (t + 1) * 128],
                            vst[:, hb * 4096 + kt * 128: hb * 4096 + (kt + 1) * 128],
                            start=(kt == 0), stop=(kt == nkt - 1)), inc=(tS if kt == nkt - 1 else None))
                pv_done = P.val(tS)
                P.wait("vector", tS, pv_done)
                P.wait("vector", vS, recip_done)
                P.do("vector", lambda e, pvps=pvps: e.tensor_scalar(sqs[:, 0:128], pvps[:, 0:128], lrow[:, 6:7], None, ALU.mult), inc=vS)
                P.wait("tensor", vS)
                tps = psT0 if unit % 2 == 0 else psT1
                P.do("tensor", lambda e, tps=tps: e.transpose(tps[:, 896:1024], sqs[:, 0:128], iden()), inc=tS)
                P.wait("vector", tS)
                P.do("vector", lambda e, h=h, j=j, tps=tps: e.tensor_copy(
                    xn[:, h * 1024 + j * 128: h * 1024 + (j + 1) * 128], tps[:, 896:1024]), inc=vS)
                unit += 1
            hmark[h] = P.val(tS)
        P.wait("tensor", vS)
        P.wait("scalar", vS)

    def out_proj(w_ap, bias0):
        def evac(oc, ps):
            P.wait("vector", tS)
            for tt in range(2):
                dst = x_res[:, oc * 1024 + tt * 512: oc * 1024 + (tt + 1) * 512]
                if bias0 is None:
                    P.do("vector", lambda e, dst=dst, ps=ps, tt=tt: e.tensor_add(dst, ps[:, tt * 512:(tt + 1) * 512], dst), inc=vS)
                else:
                    P.do("vector", lambda e, dst=dst, ps=ps, tt=tt, oc=oc: e.scalar_tensor_tensor(
                        dst, ps[:, tt * 512:(tt + 1) * 512], small[:, bias0 + oc:bias0 + oc + 1], dst, ALU.add, ALU.add), inc=vS)
        proj_fm(w_ap, list(range(16)), evac)
        P.wait("vector", vS)

    QSHARDS = [("w1s", wq1s_d, 256, HID, SL_G1),
               ("w2s", wq2s_d, 256, HID, SL_G2),
               ("w3s", wq3s_d, 704, D, SL_G3)]

    def quant_pass1():
        P.wait("vector", dI)
        P.do("vector", lambda e: e.memset(lrow[0:1, 0:16], 0.0))
        P.do("vector", lambda e: e.memset(small[:, 83:84], 0.0), inc=vS)
        for mi, (wn, _, R, C, _slot) in enumerate(QSHARDS):
            w_ap = di[wn]
            ngrp = (R + 127) // 128
            nst = C // 512
            for gi in range(ngrp):
                rws = min(128, R - gi * 128)
                for stp in range(nst):
                    P.wait("gpsimd", vS)
                    P.do("gpsimd", lambda e, w_ap=w_ap, gi=gi, stp=stp, rws=rws: e.dma_start(
                        out=scr[0:rws, 0:512], in_=w_ap[gi * 128: gi * 128 + rws, stp * 512:(stp + 1) * 512]), inc=dG, by=16)
                    P.wait("vector", dG)
                    P.do("vector", lambda e, rws=rws: e.tensor_reduce(
                        small[0:rws, 84:85], scr[0:rws, 0:512], AX, ALU.add, apply_absolute_value=True), inc=vS)
                    P.wait("vector", vS)
                    P.do("vector", lambda e, rws=rws: e.tensor_add(
                        small[0:rws, 83:84], small[0:rws, 83:84], small[0:rws, 84:85]), inc=vS)
            P.wait("tensor", vS)
            P.do("tensor", lambda e: e.matmul(psE[0:1, 0:1], constf[:, 0:1], small[:, 83:84], start=True, stop=True), inc=tS)
            P.wait("vector", tS)
            P.do("vector", lambda e, mi=mi: e.tensor_copy(lrow[0:1, 8 + mi: 9 + mi], psE[0:1, 0:1]))
            P.do("vector", lambda e: e.memset(small[:, 83:84], 0.0), inc=vS)
        P.wait("gpsimd", vS)
        P.do("gpsimd", lambda e: e.dma_start(out=gpart_d[0:1, 0:16], in_=lrow[0:1, 0:16]), inc=dG, by=16)
        P.do("gpsimd", lambda e: e.dma_start(out=dbgq_d[0:1, 0:16], in_=lrow[0:1, 0:16]), inc=dG, by=16)
        P.wait("gpsimd", dG)
        P.do("gpsimd", lambda e: e.collective_compute(
            "AllReduce", ALU.add, replica_groups=[[0, 1, 2, 3, 4, 5, 6, 7]],
            ins=[gpart_d.ap().opt()], outs=[gsum_d.ap().opt()]), inc=cc)
        return P.val(cc)

    def quant_pass2(ar_done):
        P.wait("gpsimd", cc, ar_done)
        P.wait("gpsimd", vS)
        P.do("gpsimd", lambda e: e.dma_start(out=lrow[0:1, 0:16], in_=gsum_d[0:1, 0:16]), inc=dG, by=16)
        gs_done = P.val(dG)
        P.wait("vector", dG, gs_done)
        for mi in range(3):
            P.do("vector", lambda e, mi=mi: e.tensor_scalar(
                lrow[0:1, 12 + mi:13 + mi], lrow[0:1, 8 + mi: 9 + mi], 1.0 / GN, 1e-8, ALU.mult, ALU.add), inc=vS)
            P.wait("vector", vS)
            P.do("vector", lambda e, mi=mi: e.reciprocal(lrow[0:1, 12 + mi:13 + mi], lrow[0:1, 12 + mi:13 + mi]), inc=vS)
            P.wait("vector", vS)
        P.wait("tensor", vS)
        for mi in range(3):
            P.do("tensor", lambda e, mi=mi: e.matmul(psE[:, mi:mi + 1], onesf_r(), lrow[0:1, 12 + mi:13 + mi], start=True, stop=True),
                 inc=(tS if mi == 2 else None))
        P.wait("vector", tS)
        for mi, (_, _, _, _, slot) in enumerate(QSHARDS):
            P.do("vector", lambda e, mi=mi, slot=slot: e.tensor_copy(small[:, slot:slot + 1], psE[:, mi:mi + 1]), inc=vS)
        for mi, (wn, wq_ap, R, C, slot) in enumerate(QSHARDS):
            w_ap = di[wn]
            ngrp = (R + 127) // 128
            nst = C // 512
            for gi in range(ngrp):
                rws = min(128, R - gi * 128)
                for stp in range(nst):
                    P.wait("gpsimd", vS)
                    P.wait("gpsimd", dS)
                    P.do("gpsimd", lambda e, w_ap=w_ap, gi=gi, stp=stp, rws=rws: e.dma_start(
                        out=scr[0:rws, 0:512], in_=w_ap[gi * 128: gi * 128 + rws, stp * 512:(stp + 1) * 512]), inc=dG, by=16)
                    P.wait("vector", dG)
                    P.do("vector", lambda e, rws=rws, slot=slot: e.tensor_scalar(
                        scr2[0:rws, 0:512], scr[0:rws, 0:512], small[0:rws, slot:slot + 1], None, ALU.mult), inc=vS)
                    P.wait("vector", vS)
                    P.do("vector", lambda e, rws=rws: e.tensor_single_scalar(
                        sqs[0:rws, 0:512], scr2[0:rws, 0:512], 0.5, ALU.is_ge))
                    P.do("vector", lambda e, rws=rws: e.tensor_single_scalar(
                        ptb[0:rws, 0:512], scr2[0:rws, 0:512], -0.5, ALU.is_le), inc=vS)
                    P.wait("vector", vS)
                    P.do("vector", lambda e, rws=rws: e.tensor_sub(
                        sqs[0:rws, 0:512], sqs[0:rws, 0:512], ptb[0:rws, 0:512]), inc=vS)
                    P.wait("sync", vS)
                    P.do("sync", lambda e, wq_ap=wq_ap, gi=gi, stp=stp, rws=rws: e.dma_start(
                        out=wq_ap.ap()[gi * 128: gi * 128 + rws, stp * 512:(stp + 1) * 512], in_=sqs[0:rws, 0:512]), inc=dS, by=16)
        st_done = P.val(dS)
        P.wait("gpsimd", dS, st_done)
        for s_d, g_d in [(wq1s_d, wq1_d), (wq2s_d, wq2_d), (wq3s_d, wq3_d)]:
            P.do("gpsimd", lambda e, s_d=s_d, g_d=g_d: e.collective_compute(
                "AllGather", ALU.bypass, replica_groups=[[0, 1, 2, 3, 4, 5, 6, 7]],
                ins=[s_d.ap().opt()], outs=[g_d.ap().opt()]), inc=cc)
        return P.val(cc)

    def hid_slice(s):
        return kst[:, s * 1024:(s + 1) * 1024] if s < 8 else vst[:, (s - 8) * 1024:(s - 7) * 1024]

    def ffn(wq_done, ln_done):
        P.wait("sync", cc, wq_done)
        st = Stream("sync", dP)
        for g in range(4):
            for li in range(11):
                hc = g * 11 + li
                for m in range(2):
                    w_d = wq1_d if m == 0 else wq2_d
                    def wload(e, buf, w_d=w_d, hc=hc):
                        return e.dma_start(
                            out=wsb[:, buf * 2048: buf * 2048 + 2048].rearrange("p (i c) -> p i c", c=128),
                            in_=w_d.ap()[:, hc * 128:(hc + 1) * 128].rearrange("(i p) c -> p i c", p=128))
                    buf, sem, rdy = st.load(wload)
                    P.wait("tensor", sem, rdy)
                    P.wait("tensor", aS if m == 0 else vS)
                    if g == 0 and li == 0 and m == 0:
                        P.wait("tensor", vS, ln_done)
                        P.wait("tensor", vS)
                    ps = psA if m == 0 else psB
                    for tt in range(2):
                        for i in range(NCH):
                            P.do("tensor", lambda e, buf=buf, i=i, tt=tt, ps=ps: e.matmul(
                                ps[:, tt * 512:(tt + 1) * 512],
                                wsb[:, buf * 2048 + i * 128: buf * 2048 + (i + 1) * 128],
                                xn[:, i * 1024 + tt * 512: i * 1024 + (tt + 1) * 512],
                                start=(i == 0), stop=(i == NCH - 1)), inc=(tS if i == NCH - 1 else None))
                    st.mark()
                    if m == 0:
                        P.wait("scalar", tS)
                        P.wait("scalar", vS)
                        P.do("scalar", lambda e: e.activation(sqs[:, 0:1024], psA[:, 0:1024], AF.Sigmoid), inc=aS)
                P.wait("vector", aS)
                P.wait("vector", tS)
                P.do("vector", lambda e, li=li: e.tensor_mul(hid_slice(li), sqs[:, 0:1024], psB[:, 0:1024]), inc=vS)
            for oc in range(16):
                def w3load(e, buf, g=g, oc=oc):
                    return e.dma_start(
                        out=pbuf[:, 4096 + buf * 1408: 4096 + buf * 1408 + 1408].rearrange("p (i c) -> p i c", c=128),
                        in_=wq3_d.ap()[g * 1408:(g + 1) * 1408, oc * 128:(oc + 1) * 128].rearrange("(i p) c -> p i c", p=128))
                buf, sem, rdy = st.load(w3load)
                P.wait("tensor", sem, rdy)
                P.wait("tensor", vS)
                for tt in range(2):
                    ps = psE if tt == 0 else psF
                    for i in range(11):
                        P.do("tensor", lambda e, buf=buf, i=i, tt=tt, ps=ps: e.matmul(
                            ps[:, 0:512],
                            pbuf[:, 4096 + buf * 1408 + i * 128: 4096 + buf * 1408 + (i + 1) * 128],
                            hid_slice(i)[:, tt * 512:(tt + 1) * 512],
                            start=(i == 0), stop=(i == 10)), inc=(tS if i == 10 else None))
                st.mark()
                P.wait("vector", tS)
                for tt in range(2):
                    ps = psE if tt == 0 else psF
                    dst = x_res[:, oc * 1024 + tt * 512: oc * 1024 + (tt + 1) * 512]
                    P.do("vector", lambda e, dst=dst, ps=ps: e.tensor_add(dst, ps[:, 0:512], dst), inc=vS)

    def tap(name):
        if name in TAPS:
            dbg = nc.dram_tensor("tap_" + name, [D, TOK], F32, kind="ExternalOutput").ap()
            P.wait("sync", vS)
            P.do("sync", lambda e, dbg=dbg: e.dma_start(
                out=dbg.rearrange("(i p) t -> p i t", p=128),
                in_=x_res[:].rearrange("p (i t) -> p i t", t=1024)), inc=dS, by=16)
            P.wait("vector", dS)

    loads()
    if stage >= 2:
        ar_g = quant_pass1()
    ln1 = layernorm()
    ag_a, sp_a = attn_a_prep(ln1)
    if stage >= 2:
        wq_done = quant_pass2(ar_g)
    attention("a", ag_a, sp_a)
    out_proj(woT, None)
    tap("x2")
    if stage >= 2:
        ln2 = layernorm()
        ffn(wq_done, ln2)
        tap("x3")
    if stage >= 3:
        ln3 = layernorm()
        ag_b, sp_b = attn_b_prep(ln3)
        attention("b", ag_b, sp_b)
        out_proj(wout, SL_OB0)
        tap("x4")
    if stage >= 4:
        ln4 = layernorm()
        ffn(wq_done, ln4)
    P.wait("sync", vS)
    P.do("sync", lambda e: e.dma_start(
        out=out_d.rearrange("(i p) t -> p i t", p=128),
        in_=x_res[:].rearrange("p (i t) -> p i t", t=1024)), inc=dS, by=16)
    if DUMP:
        interns = {t.name: t for t in [kA_loc, vA_loc, kA_g, vA_g, kB_loc, vB_loc, kB_g, vB_g,
                                       q_sp, qR_sp, wq1s_d, wq2s_d, wq3s_d, wq1_d, wq2_d, wq3_d,
                                       gpart_d, gsum_d, dbgq_d]}
        for nm in DUMP:
            t = interns[nm]
            dbg = nc.dram_tensor("dbg_" + nm, list(t.shape), t.dtype, kind="ExternalOutput").ap()
            P.wait("gpsimd", cc)
            P.wait("gpsimd", vS)
            P.do("gpsimd", lambda e, dbg=dbg, t=t: e.dma_start(out=dbg, in_=t.ap()), inc=dG, by=16)
        P.wait("gpsimd", dG)
    P.wait("sync", dS)

    import contextlib
    es = contextlib.ExitStack()
    for name in P.cnt:
        P.handles[name] = es.enter_context(nc.semaphore(name))
    with es, nc.Block() as block:
        @block.tensor
        def _(eng):
            for f in P.q["tensor"]:
                f(eng)

        @block.vector
        def _(eng):
            for f in P.q["vector"]:
                f(eng)

        @block.scalar
        def _(eng):
            for f in P.q["scalar"]:
                f(eng)

        @block.gpsimd
        def _(eng):
            for f in P.q["gpsimd"]:
                f(eng)

        @block.sync
        def _(eng):
            for f in P.q["sync"]:
                f(eng)
    return nc



def _host_constants():
    c = {}
    ones = np.ones((128, 128), np.float32)
    c["ones_bf"] = ones.astype(BF)
    c["ones_f"] = ones.astype(np.float32)
    c["iden_bf"] = np.eye(128, dtype=np.float32).astype(BF)
    rA = np.zeros((128, 128), np.float32)
    for i in range(64):
        rA[i, i + 64] = -1.0
        rA[i + 64, i] = 1.0
    c["rotA"] = rA.T.copy().astype(BF)
    rB = np.zeros((128, 128), np.float32)
    for i in range(64):
        rB[2 * i, 2 * i + 1] = -1.0
        rB[2 * i + 1, 2 * i] = 1.0
    c["rotB"] = rB.T.copy().astype(BF)
    return c


def _pos_own(cid):
    c = cid % 4
    j = np.arange(NJ)
    g = 4 * j + c
    return (g[:, None] * 128 + np.arange(128)[None, :]).reshape(-1).astype(np.float64)


def _tables(cid):
    t = {}
    pos = _pos_own(cid)
    inv = 1.0 / (1e6 ** (np.arange(0, 128, 2) / 128.0))
    fr = pos[None, :] * np.concatenate([inv, inv])[:, None]
    t["cosA"] = np.cos(fr).astype(np.float32)
    t["sinA"] = np.sin(fr).astype(np.float32)
    thk = np.exp(-2.0 * np.repeat(np.arange(DHR // 2), 2) / DHR * np.log(10000.0))
    frk = (pos[None, :] + 1.0) * thk[:, None]
    ck = np.zeros((128, TOK), np.float32)
    sk = np.zeros((128, TOK), np.float32)
    ck[:64] = np.cos(frk)
    sk[:64] = np.sin(frk)
    t["cosKR"] = ck
    t["sinKR"] = sk
    thq = np.exp(-2.0 * np.repeat(np.arange(512), 2) / 1024.0 * np.log(10000.0))
    cq = np.empty((128, 8 * TOK), np.float32)
    sq = np.empty((128, 8 * TOK), np.float32)
    for ch in range(8):
        frq = (pos[None, :] + 1.0) * thq[ch * 128:(ch + 1) * 128][:, None]
        cq[:, ch * TOK:(ch + 1) * TOK] = np.cos(frq)
        sq[:, ch * TOK:(ch + 1) * TOK] = np.sin(frq)
    t["cosQR"] = cq
    t["sinQR"] = sq
    m = np.zeros((128, 512), np.float32)
    c = cid % 4
    pp = c * 128 + np.arange(128)
    m[pp[:, None] < np.arange(512)[None, :]] = NEG
    t["maskD"] = m
    return t


def _bias_fm(b, nch):
    return np.ascontiguousarray(np.asarray(b).reshape(nch, 128).T).astype(np.float32)


def make_in_maps(inputs):
    g = {k: np.asarray(v) for k, v in inputs.items()}
    x = g["L0_attn_a_x"]
    shared = {
        "wqkvT": np.ascontiguousarray(g["L0_attn_a_w_qkv"].T).astype(np.float32),
        "woT": np.ascontiguousarray(g["L0_attn_a_w_o"].T).astype(np.float32),
        "qn_g": g["L0_attn_a_qn_g"].reshape(128, 1).astype(np.float32),
        "qn_b": g["L0_attn_a_qn_b"].reshape(128, 1).astype(np.float32),
        "kn_g": g["L0_attn_a_kn_g"].reshape(128, 1).astype(np.float32),
        "kn_b": g["L0_attn_a_kn_b"].reshape(128, 1).astype(np.float32),
        "wdown": np.ascontiguousarray(g["L0_attn_b_W_down_k"]).astype(np.float32),
        "wdown_b": _bias_fm(g["L0_attn_b_W_down_b"], 8),
        "wukv": np.ascontiguousarray(g["L0_attn_b_W_uKV_k"]).astype(np.float32),
        "wukv_bk": _bias_fm(g["L0_attn_b_W_uKV_b"][:D], 16),
        "wukv_bv": g["L0_attn_b_W_uKV_b"][D:].reshape(1, D).astype(np.float32),
        "wuq": np.ascontiguousarray(g["L0_attn_b_W_uQ_k"]).astype(np.float32),
        "wuq_b": _bias_fm(g["L0_attn_b_W_uQ_b"], 16),
        "wkr": np.ascontiguousarray(g["L0_attn_b_Wkr_k"]).astype(np.float32),
        "wkr_b": np.pad(np.asarray(g["L0_attn_b_Wkr_b"]), (0, 64)).reshape(1, 128).T.copy().astype(np.float32),
        "wqr": np.ascontiguousarray(g["L0_attn_b_Wqr_k"]).astype(np.float32),
        "wqr_b": _bias_fm(g["L0_attn_b_Wqr_b"], 8),
        "wout": np.ascontiguousarray(g["L0_attn_b_Wout_k"]).astype(np.float32),
        "wout_b": _bias_fm(g["L0_attn_b_Wout_b"], 16),
    }
    w1T = np.ascontiguousarray(g["L0_ffn_w1"].T).astype(np.float32)
    w2T = np.ascontiguousarray(g["L0_ffn_w2"].T).astype(np.float32)
    w3T = np.ascontiguousarray(g["L0_ffn_w3"].T).astype(np.float32)
    consts = _host_constants()

    in_maps = []
    for cid in range(8):
        b, c = cid // 4, cid % 4
        jj = 4 * np.arange(NJ) + c
        xs = x[b].reshape(32, 128, D)[jj]
        m = dict(shared)
        m.update(consts)
        m.update(_tables(cid))
        m["x_fm"] = np.ascontiguousarray(xs.reshape(TOK, D).T).astype(np.float32)
        m["w1s"] = np.ascontiguousarray(w1T[cid * 256:(cid + 1) * 256]).astype(np.float32)
        m["w2s"] = np.ascontiguousarray(w2T[cid * 256:(cid + 1) * 256]).astype(np.float32)
        m["w3s"] = np.ascontiguousarray(w3T[cid * 704:(cid + 1) * 704]).astype(np.float32)
        in_maps.append(m)
    return in_maps


def assemble(results):
    out = np.empty((B, T, D), np.float32)
    for cid in range(8):
        b, c = cid // 4, cid % 4
        oc = np.asarray(results[cid]["out"])
        jj = 4 * np.arange(NJ) + c
        out[b].reshape(32, 128, D)[jj] = oc.T.reshape(NJ, 128, D)
    return out


def kernel(**inputs):
    in_maps = make_in_maps(inputs)
    nc = bass.Bass("TRN2", target_bir_lowering=False, detect_race_conditions=False)
    build(nc, STAGE)
    res = run_bass_kernel_spmd(nc, in_maps, core_ids=list(range(8)))
    return assemble(res.results)


# version 02
# speedup vs baseline: 1.2639x; 1.2639x over previous
import math
import numpy as np
import ml_dtypes
import concourse.bass as bass
import concourse.mybir as mybir
from concourse.bass_utils import run_bass_kernel_spmd

F32 = mybir.dt.float32
BF16 = mybir.dt.bfloat16
AF = mybir.ActivationFunctionType
ALU = mybir.AluOpType
AX = mybir.AxisListType.X
BF = ml_dtypes.bfloat16

B = 2
T = 4096
D = 2048
NCH = 16
TOK = 1024
NJ = 8
HQ = 16
HKV = 4
DH = 128
HID = 5632
NHC = 44
LAT = 512
DHR = 64
SCL = 1.0 / math.sqrt(128.0)
NEG = -1.0e9
EPS_LN = 1e-5
GN = float(HID * D)

STAGE = 4
DUMP = []
TAPS = []

ENGS = ("tensor", "vector", "scalar", "gpsimd", "sync")


class Rec:
    def __init__(self):
        self.q = {e: [] for e in ENGS}
        self.cnt = {}
        self.handles = {}
        self.phase = "init"

    def wait(self, eng, name, val=None):
        v = self.cnt.get(name, 0) if val is None else val
        if v <= 0:
            return
        self.q[eng].append(lambda e, n=name, v=v: e.wait_ge(self.handles[n], v))

    def do(self, eng, fn, inc=None, by=1):
        if inc is not None:
            self.cnt[inc] = self.cnt.get(inc, 0) + by
        def f(e, fn=fn, inc=inc, by=by):
            inst = fn(e)
            if inc is not None:
                inst.then_inc(self.handles[inc], by)
        f.phase = self.phase
        self.q[eng].append(f)

    def val(self, name):
        return self.cnt.get(name, 0)


def build(nc, stage):
    P = Rec()
    di = {}

    def din(name, shape, dt):
        di[name] = nc.dram_tensor(name, shape, dt, kind="ExternalInput").ap()
        return di[name]

    x_in = din("x_fm", [D, TOK], F32)
    wqkvT = din("wqkvT", [D, 3072], F32)
    woT = din("woT", [D, D], F32)
    din("qn_g", [128, 1], F32); din("qn_b", [128, 1], F32)
    din("kn_g", [128, 1], F32); din("kn_b", [128, 1], F32)
    wdown = din("wdown", [D, 2 * LAT], F32)
    din("wdown_b", [128, 8], F32)
    wukv = din("wukv", [LAT, 2 * D], F32)
    din("wukv_bk", [128, 16], F32)
    wukv_bv = din("wukv_bv", [1, D], F32)
    wuq = din("wuq", [LAT, D], F32)
    din("wuq_b", [128, 16], F32)
    wkr = din("wkr", [D, DHR], F32)
    din("wkr_b", [128, 1], F32)
    wqr = din("wqr", [D, 1024], F32)
    din("wqr_b", [128, 8], F32)
    wout = din("wout", [D, D], F32)
    din("wout_b", [128, 16], F32)
    din("w1s", [256, HID], F32)
    din("w2s", [256, HID], F32)
    din("w3s", [704, D], F32)
    din("cosA", [128, TOK], F32)
    din("sinA", [128, TOK], F32)
    din("cosKR", [128, TOK], F32)
    din("sinKR", [128, TOK], F32)
    cosQR_d = din("cosQR", [128, 8 * TOK], F32)
    sinQR_d = din("sinQR", [128, 8 * TOK], F32)
    din("maskD", [128, 512], F32)
    din("ones_bf", [128, 128], BF16)
    din("ones_f", [128, 128], F32)
    din("iden_bf", [128, 128], BF16)
    din("rotA", [128, 128], BF16)
    din("rotB", [128, 128], BF16)

    out_d = nc.dram_tensor("out", [D, TOK], F32, kind="ExternalOutput").ap()

    kA_loc = nc.dram_tensor("kA_loc", [HKV * DH, TOK], BF16)
    vA_loc = nc.dram_tensor("vA_loc", [TOK, HKV * DH], BF16)
    kA_g = nc.dram_tensor("kA_g", [4 * HKV * DH, TOK], BF16)
    vA_g = nc.dram_tensor("vA_g", [4 * TOK, HKV * DH], BF16)
    kB_loc = nc.dram_tensor("kB_loc", [D + DHR, TOK], BF16)
    vB_loc = nc.dram_tensor("vB_loc", [TOK, D], BF16)
    kB_g = nc.dram_tensor("kB_g", [4 * (D + DHR), TOK], BF16)
    vB_g = nc.dram_tensor("vB_g", [4 * TOK, D], BF16)
    q_sp = nc.dram_tensor("q_sp", [D, TOK], BF16)
    qR_sp = nc.dram_tensor("qR_sp", [1024, TOK], BF16)
    wq1s_d = nc.dram_tensor("wq1s_d", [256, HID], BF16)
    wq2s_d = nc.dram_tensor("wq2s_d", [256, HID], BF16)
    wq3s_d = nc.dram_tensor("wq3s_d", [704, D], BF16)
    wq1_d = nc.dram_tensor("wq1_d", [D, HID], BF16)
    wq2_d = nc.dram_tensor("wq2_d", [D, HID], BF16)
    wq3_d = nc.dram_tensor("wq3_d", [HID, D], BF16)
    gpart_d = nc.dram_tensor("gpart", [1, 16], F32)
    gsum_d = nc.dram_tensor("gsum", [1, 16], F32)
    dbgq_d = nc.dram_tensor("dbgq", [1, 16], F32)

    x_res = nc.alloc_sbuf_tensor("x_res", [128, NCH * TOK], F32)
    xn = nc.alloc_sbuf_tensor("xn", [128, NCH * TOK], BF16)
    kst = nc.alloc_sbuf_tensor("kst", [128, 2 * 4096], BF16)
    vst = nc.alloc_sbuf_tensor("vst", [128, 2 * 4096], BF16)
    pbuf = nc.alloc_sbuf_tensor("pbuf", [128, 2 * 4096], BF16)
    ptb = nc.alloc_sbuf_tensor("ptb", [128, 2 * 1024], BF16)
    kRb = nc.alloc_sbuf_tensor("kRb", [128, 4096], BF16)
    qrt = nc.alloc_sbuf_tensor("qrt", [128, 4 * 1024], BF16)
    wsb = nc.alloc_sbuf_tensor("wsb", [128, 2 * 2048], BF16)
    qwork = nc.alloc_sbuf_tensor("qwork", [128, 2 * 1024], BF16)
    scr = nc.alloc_sbuf_tensor("scr", [128, 1024], F32)
    scr2 = nc.alloc_sbuf_tensor("scr2", [128, 1024], F32)
    sqs = nc.alloc_sbuf_tensor("sqs", [128, 1024], BF16)
    maskb = nc.alloc_sbuf_tensor("maskb", [128, 512], F32)
    consts = nc.alloc_sbuf_tensor("consts", [128, 4 * 128], BF16)
    constf = nc.alloc_sbuf_tensor("constf", [128, 128], F32)
    small = nc.alloc_sbuf_tensor("small", [128, 96], F32)
    lrow = nc.alloc_sbuf_tensor("lrow", [128, 16], F32)

    pbf = pbuf[:].bitcast(F32)
    rows = lambda off, n: pbf[0:1, off:off + n]
    tbl = lambda off, n: pbf[:, 2048 + off: 2048 + off + n]

    SL_QG, SL_QB, SL_KG, SL_KB = 0, 1, 2, 3
    SL_DWB0 = 8
    SL_UKB0 = 16
    SL_UQB0 = 32
    SL_KRB = 48
    SL_QRB0 = 49
    SL_OB0 = 57
    SL_G1, SL_G2, SL_G3 = 80, 81, 82
    nc.const_aps.register(F32, EPS_LN, small[:, 90:91])

    psA = nc.alloc_psum_tensor("psA", [128, 1024], F32)
    psB = nc.alloc_psum_tensor("psB", [128, 1024], F32)
    psT0 = nc.alloc_psum_tensor("psT0", [128, 1024], BF16)
    psT1 = nc.alloc_psum_tensor("psT1", [128, 1024], BF16)
    psE = nc.alloc_psum_tensor("psE", [128, 512], F32)
    psF = nc.alloc_psum_tensor("psF", [128, 512], F32)

    onesb1 = lambda: consts[:, 0:1]
    iden = lambda: consts[:, 128:256]
    rotAc = lambda: consts[:, 256:384]
    rotBc = lambda: consts[:, 384:512]
    onesf_r = lambda: constf[0:1, 0:128]

    dI, dS, dG, cc = "dI", "dS", "dG", "cc"
    tS, vS, aS, pS = "tS", "vS", "aS", "pS"
    dP = ("dP0", "dP1")
    dQ = ("dQ0", "dQ1")

    def loads():
        P.do("sync", lambda e: e.dma_start(out=x_res[:].rearrange("p (i t) -> p i t", t=1024),
                                           in_=x_in.rearrange("(i p) t -> p i t", p=128)), inc=dI, by=16)
        P.do("sync", lambda e: e.dma_start(out=consts[:, 0:128], in_=di["ones_bf"]), inc=dI, by=16)
        P.do("sync", lambda e: e.dma_start(out=consts[:, 128:256], in_=di["iden_bf"]), inc=dI, by=16)
        P.do("sync", lambda e: e.dma_start(out=consts[:, 256:384], in_=di["rotA"]), inc=dI, by=16)
        P.do("sync", lambda e: e.dma_start(out=consts[:, 384:512], in_=di["rotB"]), inc=dI, by=16)
        P.do("sync", lambda e: e.dma_start(out=constf[:], in_=di["ones_f"]), inc=dI, by=16)
        P.do("sync", lambda e: e.dma_start(out=maskb[:], in_=di["maskD"]), inc=dI, by=16)
        for nm, col, n in [("qn_g", SL_QG, 1), ("qn_b", SL_QB, 1), ("kn_g", SL_KG, 1), ("kn_b", SL_KB, 1),
                           ("wdown_b", SL_DWB0, 8), ("wukv_bk", SL_UKB0, 16), ("wuq_b", SL_UQB0, 16),
                           ("wkr_b", SL_KRB, 1), ("wqr_b", SL_QRB0, 8), ("wout_b", SL_OB0, 16)]:
            P.do("sync", lambda e, nm=nm, col=col, n=n: e.dma_start(out=small[:, col:col + n], in_=di[nm]), inc=dI, by=16)
        ld = P.val(dI)
        P.wait("vector", dI, ld)
        P.do("vector", lambda e: e.memset(small[:, 90:91], EPS_LN))
        P.do("vector", lambda e: e.tensor_single_scalar(small[:, SL_QG:SL_QG + 1], small[:, SL_QG:SL_QG + 1], SCL, ALU.mult))
        P.do("vector", lambda e: e.tensor_single_scalar(small[:, SL_QB:SL_QB + 1], small[:, SL_QB:SL_QB + 1], SCL, ALU.mult), inc=vS)
        return ld

    def load_tables(cos_ap, sin_ap):
        P.wait("sync", tS)
        P.wait("sync", aS)
        P.wait("sync", vS)
        P.do("sync", lambda e: e.dma_start(out=tbl(0, 1024), in_=cos_ap), inc=dI, by=16)
        P.do("sync", lambda e: e.dma_start(out=tbl(1024, 1024), in_=sin_ap), inc=dI, by=16)
        return P.val(dI)

    def layernorm():
        P.wait("gpsimd", dI)
        P.wait("gpsimd", vS)
        for th in range(2):
            o = th * 512
            lnmark = {}
            for i in range(NCH):
                par = i % 2
                if i >= 2:
                    P.wait("gpsimd", tS, lnmark[i - 2])
                P.do("gpsimd", lambda e, i=i, o=o, par=par: e.tensor_copy(sqs[:, par * 512:par * 512 + 512], x_res[:, i * 1024 + o: i * 1024 + o + 512]), inc=pS)
                P.wait("scalar", pS)
                P.do("scalar", lambda e, par=par: e.activation(ptb[:, par * 512:par * 512 + 512], sqs[:, par * 512:par * 512 + 512], AF.Square), inc=aS)
                P.wait("tensor", pS)
                first, last = (i == 0), (i == NCH - 1)
                if first:
                    P.wait("tensor", vS)
                P.do("tensor", lambda e, first=first, last=last, par=par: e.matmul(psA[0:1, 0:512], onesb1(), sqs[:, par * 512:par * 512 + 512], start=first, stop=last))
                P.wait("tensor", aS)
                P.do("tensor", lambda e, first=first, last=last, par=par: e.matmul(psA[0:1, 512:1024], onesb1(), ptb[:, par * 512:par * 512 + 512], start=first, stop=last), inc=tS)
                lnmark[i] = P.val(tS)
            P.wait("vector", tS)
            P.do("vector", lambda e: e.tensor_single_scalar(rows(0, 512), psA[0:1, 0:512], 1.0 / D, ALU.mult))
            P.do("vector", lambda e: e.tensor_single_scalar(rows(1536, 512), psA[0:1, 512:1024], 1.0 / D, ALU.mult), inc=vS)
            P.wait("scalar", vS)
            P.do("scalar", lambda e: e.activation(rows(512, 512), rows(0, 512), AF.Square), inc=aS)
            P.wait("vector", aS)
            P.do("vector", lambda e: e.tensor_sub(rows(512, 512), rows(1536, 512), rows(512, 512)), inc=vS)
            P.wait("scalar", vS)
            P.do("scalar", lambda e: e.activation(rows(512, 512), rows(512, 512), AF.Sqrt, bias=EPS_LN), inc=aS)
            P.wait("vector", aS)
            P.do("vector", lambda e: e.reciprocal(rows(512, 512), rows(512, 512)), inc=vS)
            P.wait("vector", vS)
            P.do("vector", lambda e: e.scalar_tensor_tensor(rows(1024, 512), rows(0, 512), -1.0, rows(512, 512), ALU.mult, ALU.mult), inc=vS)
            P.wait("tensor", vS)
            P.do("tensor", lambda e: e.matmul(psA[:, 0:512], onesf_r(), rows(512, 512), start=True, stop=True))
            P.do("tensor", lambda e: e.matmul(psA[:, 512:1024], onesf_r(), rows(1024, 512), start=True, stop=True), inc=tS)
            P.wait("vector", tS)
            for i in range(NCH):
                par = i % 2
                P.do("vector", lambda e, i=i, o=o, par=par: e.tensor_mul(scr[:, par * 512:par * 512 + 512], x_res[:, i * 1024 + o: i * 1024 + o + 512], psA[:, 0:512]), inc=vS)
                P.do("vector", lambda e, i=i, o=o, par=par: e.tensor_add(xn[:, i * 1024 + o: i * 1024 + o + 512], scr[:, par * 512:par * 512 + 512], psA[:, 512:1024]), inc=vS)
        P.wait("vector", vS)
        return P.val(vS)

    class Stream:

        def __init__(self, queue, sems):
            self.queue = queue
            self.sems = sems
            self.idx = 0
            self.marks = []
            self.entry = P.val(tS)

        def load(self, fn):
            buf = self.idx % 2
            if self.idx >= 2:
                P.wait(self.queue, tS, self.marks[self.idx - 2])
            else:
                P.wait(self.queue, tS, self.entry)
            sem = self.sems[buf]
            P.do(self.queue, lambda e, fn=fn, buf=buf: fn(e, buf), inc=sem, by=16)
            return buf, sem, P.val(sem)

        def mark(self):
            self.marks.append(P.val(tS))
            self.idx += 1

    def wsb_block(w_ap, oc, nin=NCH):
        def fn(e, buf):
            return e.dma_start(
                out=wsb[:, buf * 2048: buf * 2048 + nin * 128].rearrange("p (i c) -> p i c", c=128),
                in_=w_ap[:, oc * 128:(oc + 1) * 128].rearrange("(i p) c -> p i c", p=128))
        return fn

    def proj_fm(w_ap, out_chunks, evac, nin=NCH, extra_wait=None):
        st = Stream("gpsimd", dQ)
        for bi, oc in enumerate(out_chunks):
            buf, sem, rdy = st.load(wsb_block(w_ap, oc, nin))
            P.wait("tensor", sem, rdy)
            if extra_wait is not None:
                P.wait("tensor", extra_wait[0], extra_wait[1])
            P.wait("tensor", vS)
            ps = psA if bi % 2 == 0 else psB
            for tt in range(2):
                for i in range(nin):
                    P.do("tensor", lambda e, buf=buf, i=i, tt=tt, ps=ps, nin=nin: e.matmul(
                        ps[:, tt * 512:(tt + 1) * 512],
                        wsb[:, buf * 2048 + i * 128: buf * 2048 + (i + 1) * 128],
                        xn[:, i * 1024 + tt * 512: i * 1024 + (tt + 1) * 512],
                        start=(i == 0), stop=(i == nin - 1)), inc=(tS if i == nin - 1 else None))
            st.mark()
            evac(oc, ps)

    def head_stats(src, war_mark):
        P.wait("scalar", vS)
        P.do("scalar", lambda e, src=src: e.activation(sqs[:, 0:1024], src, AF.Square), inc=aS)
        P.wait("tensor", aS)
        P.wait("tensor", vS, war_mark)
        P.do("tensor", lambda e, src=src: e.matmul(psA[0:1, 0:512], onesb1(), src[:, 0:512], start=True, stop=True))
        P.do("tensor", lambda e, src=src: e.matmul(psA[0:1, 512:1024], onesb1(), src[:, 512:1024], start=True, stop=True))
        P.do("tensor", lambda e: e.matmul(psB[0:1, 0:512], onesb1(), sqs[:, 0:512], start=True, stop=True))
        P.do("tensor", lambda e: e.matmul(psB[0:1, 512:1024], onesb1(), sqs[:, 512:1024], start=True, stop=True), inc=tS)
        P.wait("vector", tS)
        P.do("vector", lambda e: e.tensor_single_scalar(rows(0, 1024), psA[0:1, 0:1024], 1.0 / 128, ALU.mult))
        P.do("vector", lambda e: e.tensor_single_scalar(rows(1024, 1024), psB[0:1, 0:1024], 1.0 / 128, ALU.mult), inc=vS)
        P.wait("scalar", vS)
        P.do("scalar", lambda e: e.activation(scr2[0:1, 0:1024], rows(0, 1024), AF.Square), inc=aS)
        P.wait("vector", aS)
        P.do("vector", lambda e: e.tensor_sub(rows(1024, 1024), rows(1024, 1024), scr2[0:1, 0:1024]), inc=vS)
        P.wait("scalar", vS)
        P.do("scalar", lambda e: e.activation(rows(1024, 1024), rows(1024, 1024), AF.Sqrt, bias=EPS_LN), inc=aS)
        P.wait("vector", aS)
        P.do("vector", lambda e: e.reciprocal(rows(1024, 1024), rows(1024, 1024)), inc=vS)
        P.wait("vector", vS)
        P.do("vector", lambda e: e.scalar_tensor_tensor(rows(0, 1024), rows(0, 1024), -1.0, rows(1024, 1024), ALU.mult, ALU.mult), inc=vS)
        P.wait("tensor", vS)
        for half in range(2):
            P.do("tensor", lambda e, half=half: e.matmul(psA[:, half * 512:(half + 1) * 512], onesf_r(), rows(1024 + half * 512, 512), start=True, stop=True))
            P.do("tensor", lambda e, half=half: e.matmul(psB[:, half * 512:(half + 1) * 512], onesf_r(), rows(half * 512, 512), start=True, stop=True),
                 inc=(tS if half == 1 else None))

    def rope_apply(use_rotA, par_out, nrows=128):
        nr = nrows
        P.wait("gpsimd", vS)
        P.do("gpsimd", lambda e, nr=nr: e.tensor_copy(sqs[0:nr, 0:1024], scr[0:nr, 0:1024]), inc=pS)
        P.wait("tensor", pS)
        P.wait("tensor", vS)
        rot = rotAc if use_rotA else rotBc
        P.do("tensor", lambda e, rot=rot, nr=nr: e.matmul(psB[0:nr, 0:512], rot()[0:nr, 0:nr], sqs[0:nr, 0:512], start=True, stop=True))
        P.do("tensor", lambda e, rot=rot, nr=nr: e.matmul(psB[0:nr, 512:1024], rot()[0:nr, 0:nr], sqs[0:nr, 512:1024], start=True, stop=True), inc=tS)
        P.wait("vector", tS)
        P.do("vector", lambda e, nr=nr: e.tensor_mul(scr2[0:nr, 0:1024], psB[0:nr, 0:1024], tbl(1024, 1024)[0:nr, :]), inc=vS)
        P.do("vector", lambda e, nr=nr: e.tensor_mul(scr[0:nr, 0:1024], scr[0:nr, 0:1024], tbl(0, 1024)[0:nr, :]), inc=vS)
        P.wait("vector", vS)
        P.do("vector", lambda e, par=par_out, nr=nr: e.tensor_add(qwork[0:nr, par * 1024:(par + 1) * 1024], scr[0:nr, 0:1024], scr2[0:nr, 0:1024]), inc=vS)

    def attn_a_prep(ln_done):
        tb_done = load_tables(di["cosA"], di["sinA"])
        P.wait("vector", dI, tb_done)
        st = Stream("gpsimd", dQ)
        evac_mark = 0
        rope_mark = 0
        for idx in range(20):
            is_q = idx < 16
            oc = idx
            par = idx % 2
            buf, sem, rdy = st.load(wsb_block(wqkvT, oc))
            P.wait("tensor", sem, rdy)
            P.wait("tensor", vS, ln_done)
            P.wait("tensor", vS, evac_mark)
            for tt in range(2):
                ps = psE if tt == 0 else psF
                for i in range(NCH):
                    P.do("tensor", lambda e, buf=buf, i=i, ps=ps, tt=tt: e.matmul(
                        ps[:, 0:512],
                        wsb[:, buf * 2048 + i * 128: buf * 2048 + (i + 1) * 128],
                        xn[:, i * 1024 + tt * 512: i * 1024 + (tt + 1) * 512],
                        start=(i == 0), stop=(i == NCH - 1)), inc=(tS if i == NCH - 1 else None))
            st.mark()
            P.wait("vector", tS)
            P.wait("vector", dS)
            P.do("vector", lambda e, par=par: e.tensor_copy(qwork[:, par * 1024: par * 1024 + 512], psE[:, 0:512]))
            P.do("vector", lambda e, par=par: e.tensor_copy(qwork[:, par * 1024 + 512: par * 1024 + 1024], psF[:, 0:512]), inc=vS)
            evac_mark = P.val(vS)
            src = qwork[:, par * 1024:(par + 1) * 1024]
            head_stats(src, rope_mark)
            g_slot = SL_QG if is_q else SL_KG
            b_slot = SL_QB if is_q else SL_KB
            P.wait("vector", tS)
            P.do("vector", lambda e, src=src: e.tensor_mul(scr[:, 0:1024], src, psA[:, 0:1024]), inc=vS)
            P.do("vector", lambda e: e.tensor_add(scr[:, 0:1024], scr[:, 0:1024], psB[:, 0:1024]), inc=vS)
            P.do("vector", lambda e, g=g_slot, b=b_slot: e.tensor_scalar(
                scr[:, 0:1024], scr[:, 0:1024], small[:, g:g + 1], small[:, b:b + 1], ALU.mult, ALU.add), inc=vS)
            rope_apply(True, par)
            rope_mark = P.val(vS)
            P.wait("sync", vS)
            if is_q:
                P.do("sync", lambda e, oc=oc, par=par: e.dma_start(
                    out=q_sp.ap()[oc * 128:(oc + 1) * 128, :], in_=qwork[:, par * 1024:(par + 1) * 1024]), inc=dS, by=16)
            else:
                P.do("sync", lambda e, oc=oc, par=par: e.dma_start(
                    out=kA_loc.ap()[(oc - 16) * 128:(oc - 15) * 128, :], in_=qwork[:, par * 1024:(par + 1) * 1024]), inc=dS, by=16)
        for vb in range(4):
            def vload(e, buf, vb=vb):
                return e.dma_start(
                    out=wsb[:, buf * 2048: buf * 2048 + 2048].rearrange("p (i c) -> p i c", c=128),
                    in_=wqkvT[:, 2560 + vb * 128: 2560 + (vb + 1) * 128].rearrange("(i p) c -> p i c", p=128))
            buf, sem, rdy = st.load(vload)
            P.wait("tensor", sem, rdy)
            for ts in range(NJ):
                ps = psE if ts % 2 == 0 else psF
                P.wait("tensor", vS)
                for i in range(NCH):
                    P.do("tensor", lambda e, ts=ts, i=i, ps=ps, buf=buf: e.matmul(
                        ps[:, 0:128],
                        xn[:, i * 1024 + ts * 128: i * 1024 + (ts + 1) * 128],
                        wsb[:, buf * 2048 + i * 128: buf * 2048 + (i + 1) * 128],
                        start=(i == 0), stop=(i == NCH - 1)), inc=(tS if i == NCH - 1 else None))
                P.wait("vector", tS)
                P.do("vector", lambda e, ts=ts, vb=vb, ps=ps: e.tensor_copy(
                    vst[:, ts * 512 + vb * 128: ts * 512 + (vb + 1) * 128], ps[:, 0:128]), inc=vS)
            st.mark()
        P.wait("sync", vS)
        P.do("sync", lambda e: e.dma_start(
            out=vA_loc.ap().rearrange("(ts p) c -> p ts c", p=128),
            in_=vst[:, 0:4096].rearrange("p (ts c) -> p ts c", c=512)), inc=dS, by=16)
        sp_done = P.val(dS)
        P.wait("gpsimd", dS, sp_done)
        P.do("gpsimd", lambda e: e.collective_compute(
            "AllGather", ALU.bypass, replica_groups=[[0, 1, 2, 3], [4, 5, 6, 7]],
            ins=[kA_loc.ap().opt()], outs=[kA_g.ap().opt()]), inc=cc)
        P.do("gpsimd", lambda e: e.collective_compute(
            "AllGather", ALU.bypass, replica_groups=[[0, 1, 2, 3], [4, 5, 6, 7]],
            ins=[vA_loc.ap().opt()], outs=[vA_g.ap().opt()]), inc=cc)
        return P.val(cc), sp_done

    def attn_b_prep(ln_done):
        def evac_lat(oc, ps):
            P.wait("vector", tS)
            for tt in range(2):
                P.do("vector", lambda e, oc=oc, tt=tt, ps=ps: e.tensor_scalar(
                    kst[:, oc * 1024 + tt * 512: oc * 1024 + (tt + 1) * 512],
                    ps[:, tt * 512:(tt + 1) * 512], small[:, SL_DWB0 + oc:SL_DWB0 + oc + 1], None, ALU.add), inc=vS)
        proj_fm(wdown, list(range(8)), evac_lat, extra_wait=(vS, ln_done))
        lat_done = P.val(vS)
        cKV = kst[:, 0:4096]
        cQ = kst[:, 4096:8192]

        tb2 = load_tables(di["cosKR"], di["sinKR"])
        P.wait("vector", dI, tb2)
        st = Stream("gpsimd", dQ)
        evac_mark = P.val(vS)
        rope_mark = 0
        def krload(e, buf):
            return e.dma_start(
                out=wsb[:, buf * 2048: buf * 2048 + NCH * 64].rearrange("p (i c) -> p i c", c=64),
                in_=wkr.rearrange("(i p) c -> p i c", p=128))
        buf, sem, rdy = st.load(krload)
        P.wait("tensor", sem, rdy)
        P.wait("tensor", vS, evac_mark)
        for tt in range(2):
            ps = psE if tt == 0 else psF
            for i in range(NCH):
                P.do("tensor", lambda e, buf=buf, i=i, ps=ps, tt=tt: e.matmul(
                    ps[0:64, 0:512],
                    wsb[:, buf * 2048 + i * 64: buf * 2048 + (i + 1) * 64],
                    xn[:, i * 1024 + tt * 512: i * 1024 + (tt + 1) * 512],
                    start=(i == 0), stop=(i == NCH - 1)), inc=(tS if i == NCH - 1 else None))
        st.mark()
        P.wait("vector", tS)
        for tt in range(2):
            ps = psE if tt == 0 else psF
            P.do("vector", lambda e, tt=tt, ps=ps: e.tensor_scalar(
                scr[0:64, tt * 512:(tt + 1) * 512], ps[0:64, 0:512], small[0:64, SL_KRB:SL_KRB + 1], None, ALU.add), inc=vS)
        evac_mark = P.val(vS)
        rope_apply(False, 0, nrows=64)
        rope_mark = P.val(vS)
        P.wait("sync", vS)
        P.do("sync", lambda e: e.dma_start(out=kB_loc.ap()[D:D + DHR, :], in_=qwork[0:64, 0:1024]), inc=dS, by=16)

        for ch in range(8):
            tbq = load_tables(cosQR_d[:, ch * 1024:(ch + 1) * 1024], sinQR_d[:, ch * 1024:(ch + 1) * 1024])
            P.wait("vector", dI, tbq)
            buf, sem, rdy = st.load(wsb_block(wqr, ch))
            P.wait("tensor", sem, rdy)
            P.wait("tensor", vS, evac_mark)
            for tt in range(2):
                ps = psE if tt == 0 else psF
                for i in range(NCH):
                    P.do("tensor", lambda e, buf=buf, i=i, ps=ps, tt=tt: e.matmul(
                        ps[:, 0:512],
                        wsb[:, buf * 2048 + i * 128: buf * 2048 + (i + 1) * 128],
                        xn[:, i * 1024 + tt * 512: i * 1024 + (tt + 1) * 512],
                        start=(i == 0), stop=(i == NCH - 1)), inc=(tS if i == NCH - 1 else None))
            st.mark()
            P.wait("vector", tS)
            P.wait("vector", dS)
            for tt in range(2):
                ps = psE if tt == 0 else psF
                P.do("vector", lambda e, tt=tt, ps=ps, ch=ch: e.tensor_scalar(
                    scr[:, tt * 512:(tt + 1) * 512], ps[:, 0:512],
                    small[:, SL_QRB0 + ch:SL_QRB0 + ch + 1], SCL, ALU.add, ALU.mult), inc=vS)
            evac_mark = P.val(vS)
            rope_apply(False, 0)
            rope_mark = P.val(vS)
            P.wait("sync", vS)
            P.do("sync", lambda e, ch=ch: e.dma_start(out=qR_sp.ap()[ch * 128:(ch + 1) * 128, :], in_=qwork[:, 0:1024]), inc=dS, by=16)

        for which in ("k", "q"):
            w_ap = wukv if which == "k" else wuq
            src = cKV if which == "k" else cQ
            for oc in range(16):
                def kqload(e, buf, oc=oc, w_ap=w_ap):
                    return e.dma_start(
                        out=wsb[:, buf * 2048: buf * 2048 + 4 * 128].rearrange("p (i c) -> p i c", c=128),
                        in_=w_ap[:, oc * 128:(oc + 1) * 128].rearrange("(i p) c -> p i c", p=128))
                buf, sem, rdy = st.load(kqload)
                P.wait("tensor", sem, rdy)
                P.wait("tensor", vS, lat_done)
                P.wait("tensor", vS, evac_mark)
                for tt in range(2):
                    ps = psE if tt == 0 else psF
                    for i in range(4):
                        P.do("tensor", lambda e, buf=buf, i=i, ps=ps, tt=tt, src=src: e.matmul(
                            ps[:, 0:512],
                            wsb[:, buf * 2048 + i * 128: buf * 2048 + (i + 1) * 128],
                            src[:, i * 1024 + tt * 512: i * 1024 + (tt + 1) * 512],
                            start=(i == 0), stop=(i == 3)), inc=(tS if i == 3 else None))
                st.mark()
                P.wait("vector", tS)
                P.wait("vector", dS)
                par = oc % 2
                for tt in range(2):
                    ps = psE if tt == 0 else psF
                    if False:
                        pass
                    elif which == "k":
                        P.do("vector", lambda e, tt=tt, ps=ps, oc=oc, par=par: e.tensor_scalar(
                            qwork[:, par * 1024 + tt * 512: par * 1024 + (tt + 1) * 512],
                            ps[:, 0:512], small[:, SL_UKB0 + oc:SL_UKB0 + oc + 1], None, ALU.add), inc=vS)
                    else:
                        P.do("vector", lambda e, tt=tt, ps=ps, oc=oc, par=par: e.tensor_scalar(
                            qwork[:, par * 1024 + tt * 512: par * 1024 + (tt + 1) * 512],
                            ps[:, 0:512], small[:, SL_UQB0 + oc:SL_UQB0 + oc + 1], SCL, ALU.add, ALU.mult), inc=vS)
                evac_mark = P.val(vS)
                P.wait("sync", vS)
                dst = kB_loc.ap()[oc * 128:(oc + 1) * 128, :] if which == "k" else q_sp.ap()[oc * 128:(oc + 1) * 128, :]
                P.do("sync", lambda e, dst=dst, par=par: e.dma_start(out=dst, in_=qwork[:, par * 1024:(par + 1) * 1024]), inc=dS, by=16)

        vmarks = {}
        P.wait("gpsimd", vS)
        P.do("gpsimd", lambda e: e.dma_start(out=qrt[0:1, 0:2048], in_=wukv_bv), inc=dG, by=16)
        bv_done = P.val(dG)
        for vb in range(16):
            def vload(e, buf, vb=vb):
                return e.dma_start(
                    out=wsb[:, buf * 2048: buf * 2048 + 4 * 128].rearrange("p (i c) -> p i c", c=128),
                    in_=wukv[:, D + vb * 128: D + (vb + 1) * 128].rearrange("(i p) c -> p i c", p=128))
            buf, sem, rdy = st.load(vload)
            P.wait("tensor", sem, rdy)
            P.wait("tensor", dG, bv_done)
            off = (vb % 2) * 1024
            for ts in range(NJ):
                ps = psE if ts % 2 == 0 else psF
                prev = vmarks.get((vb, ts - 2), vmarks.get((vb - 1, 6 + ts), evac_mark))
                P.wait("tensor", vS, prev)
                for i in range(4):
                    P.do("tensor", lambda e, ts=ts, i=i, ps=ps, buf=buf: e.matmul(
                        ps[:, 0:128],
                        cKV[:, i * 1024 + ts * 128: i * 1024 + (ts + 1) * 128],
                        wsb[:, buf * 2048 + i * 128: buf * 2048 + (i + 1) * 128],
                        start=(i == 0), stop=False))
                P.do("tensor", lambda e, ps=ps, vb=vb: e.matmul(
                    ps[:, 0:128], consts[0:1, 0:128],
                    qrt[0:1, vb * 128:(vb + 1) * 128],
                    start=False, stop=True), inc=tS)
                P.wait("vector", tS)
                if ts == 0:
                    P.wait("vector", dS)
                P.do("vector", lambda e, ts=ts, off=off, ps=ps: e.tensor_copy(
                    vst[:, off + ts * 128: off + (ts + 1) * 128], ps[:, 0:128]), inc=vS)
                vmarks[(vb, ts)] = P.val(vS)
            st.mark()
            P.wait("sync", vS)
            P.do("sync", lambda e, vb=vb, off=off: e.dma_start(
                out=vB_loc.ap().rearrange("(ts p) c -> p ts c", p=128)[:, :, vb * 128:(vb + 1) * 128],
                in_=vst[:, off:off + 1024].rearrange("p (ts c) -> p ts c", c=128)), inc=dS, by=16)
        sp_done = P.val(dS)
        P.wait("gpsimd", dS, sp_done)
        P.do("gpsimd", lambda e: e.collective_compute(
            "AllGather", ALU.bypass, replica_groups=[[0, 1, 2, 3], [4, 5, 6, 7]],
            ins=[kB_loc.ap().opt()], outs=[kB_g.ap().opt()]), inc=cc)
        P.do("gpsimd", lambda e: e.collective_compute(
            "AllGather", ALU.bypass, replica_groups=[[0, 1, 2, 3], [4, 5, 6, 7]],
            ins=[vB_loc.ap().opt()], outs=[vB_g.ap().opt()]), inc=cc)
        return P.val(cc), sp_done

    def attention(kind, ag_done, sp_done):
        kg = kA_g if kind == "a" else kB_g
        vg = vA_g if kind == "a" else vB_g
        KROWS = HKV * DH if kind == "a" else D + DHR
        exp_seen = {0: P.val(aS), 1: P.val(aS)}
        hmark = {}
        entry = P.val(tS)
        if kind == "b":
            P.wait("sync", cc, ag_done)
            P.wait("sync", tS, entry)
            for r in range(4):
                P.do("sync", lambda e, r=r: e.dma_start(
                    out=kRb[0:64, :].rearrange("p (g c) -> p g c", c=128)[:, r::4, :],
                    in_=kg.ap()[r * KROWS + D: r * KROWS + D + DHR, :].rearrange("p (g c) -> p g c", c=128)), inc=dI, by=16)
            kr_ready = P.val(dI)
            P.wait("tensor", dI, kr_ready)

        units = [(h, j) for h in range(16) for j in range(NJ)]
        kvr = {}
        ustate = {}
        ptb_mark = {0: 0, 1: 0}
        otmark = {0: 0, 1: 0}
        oevac_mark = {}

        def head_loads(h):
            hb = h % 2
            kh = (h // 4) if kind == "a" else h
            if h == 0:
                P.wait("sync", cc, ag_done)
                P.wait("sync", dS, sp_done)
            P.wait("sync", tS, hmark[h - 2] if h >= 2 else entry)
            sem = dP[hb]
            for r in range(4):
                P.do("sync", lambda e, r=r, hb=hb, kh=kh: e.dma_start(
                    out=kst[:, hb * 4096:(hb + 1) * 4096].rearrange("p (g c) -> p g c", c=128)[:, r::4, :],
                    in_=kg.ap()[r * KROWS + kh * 128: r * KROWS + (kh + 1) * 128, :].rearrange("p (g c) -> p g c", c=128)), inc=sem, by=16)
                P.do("sync", lambda e, r=r, hb=hb, kh=kh: e.dma_start(
                    out=vst[:, hb * 4096:(hb + 1) * 4096].rearrange("p (g c) -> p g c", c=128)[:, r::4, :],
                    in_=vg.ap().rearrange("(r g p) c -> r g p c", r=4, p=128)[r, :, :, kh * 128:(kh + 1) * 128].rearrange("g p c -> p g c")), inc=sem, by=16)
            P.do("sync", lambda e, h=h, hb=hb: e.dma_start(
                out=qrt[:, hb * 2048: hb * 2048 + 1024], in_=q_sp.ap()[h * 128:(h + 1) * 128, :]), inc=sem, by=16)
            if kind == "b":
                P.do("sync", lambda e, h=h, hb=hb: e.dma_start(
                    out=qrt[0:64, hb * 2048 + 1024: hb * 2048 + 2048], in_=qR_sp.ap()[h * 64:(h + 1) * 64, :]), inc=sem, by=16)
            return sem, P.val(sem)

        def part1(u):
            h, j = units[u]
            hb = h % 2
            if j == 0:
                kvr[h] = head_loads(h)
            sem, kv_ready = kvr[h]
            lb = (u % 2) * 8
            pref = (j + 1) * 512
            nblk = (pref + 1023) // 1024
            pb_off = (u % 2) * 4096
            for blk in range(nblk):
                c0 = blk * 1024
                bw = min(1024, pref - c0)
                rsel = (u + blk) % 2
                reg = psA if rsel == 0 else psB
                P.wait("tensor", sem, kv_ready)
                P.wait("tensor", aS, exp_seen[rsel])
                nch_here = (bw + 511) // 512
                for ch in range(nch_here):
                    cw = min(512, bw - ch * 512)
                    lastA = (kind == "a") and (ch == nch_here - 1)
                    P.do("tensor", lambda e, hb=hb, j=j, c0=c0, ch=ch, cw=cw, reg=reg: e.matmul(
                        reg[:, ch * 512: ch * 512 + cw],
                        qrt[:, hb * 2048 + j * 128: hb * 2048 + (j + 1) * 128],
                        kst[:, hb * 4096 + c0 + ch * 512: hb * 4096 + c0 + ch * 512 + cw],
                        start=True, stop=(kind == "a")), inc=(tS if lastA else None))
                if kind == "b":
                    for ch in range(nch_here):
                        cw = min(512, bw - ch * 512)
                        P.do("tensor", lambda e, j=j, hb=hb, c0=c0, ch=ch, cw=cw, reg=reg: e.matmul(
                            reg[:, ch * 512: ch * 512 + cw],
                            qrt[0:64, hb * 2048 + 1024 + j * 128: hb * 2048 + 1024 + (j + 1) * 128],
                            kRb[0:64, c0 + ch * 512: c0 + ch * 512 + cw],
                            start=False, stop=True), inc=(tS if ch == nch_here - 1 else None))
                s_done = P.val(tS)
                if c0 + bw == pref:
                    off = (pref - 512) - c0
                    P.wait("vector", tS, s_done)
                    P.do("vector", lambda e, reg=reg, off=off: e.tensor_add(reg[:, off:off + 512], reg[:, off:off + 512], maskb[:]), inc=vS)
                P.wait("vector", tS, s_done)
                P.do("vector", lambda e, reg=reg, bw=bw, lb=lb: e.tensor_reduce(lrow[:, lb + 2:lb + 3], reg[:, 0:bw], AX, ALU.max, negate=True), inc=vS)
                if blk == 0:
                    P.wait("vector", vS)
                    P.do("vector", lambda e, lb=lb: e.tensor_copy(lrow[:, lb:lb + 1], lrow[:, lb + 2:lb + 3]), inc=vS)
                    P.wait("scalar", vS)
                    P.do("scalar", lambda e, reg=reg, bw=bw, pb=pb_off, c0=c0, lb=lb: e.activation(
                        pbuf[:, pb + c0: pb + c0 + bw], reg[:, 0:bw], AF.Exp,
                        bias=lrow[:, lb:lb + 1], accum_out=lrow[:, lb + 1:lb + 2]), inc=aS)
                else:
                    P.wait("vector", vS)
                    P.do("vector", lambda e, lb=lb: e.tensor_copy(lrow[:, lb + 3:lb + 4], lrow[:, lb:lb + 1]))
                    P.do("vector", lambda e, lb=lb: e.tensor_tensor(lrow[:, lb:lb + 1], lrow[:, lb:lb + 1], lrow[:, lb + 2:lb + 3], ALU.min), inc=vS)
                    P.wait("vector", vS)
                    P.do("vector", lambda e, lb=lb: e.tensor_sub(lrow[:, lb + 3:lb + 4], lrow[:, lb:lb + 1], lrow[:, lb + 3:lb + 4]), inc=vS)
                    P.wait("scalar", vS)
                    P.do("scalar", lambda e, lb=lb: e.activation(lrow[:, lb + 4:lb + 5], lrow[:, lb + 3:lb + 4], AF.Exp), inc=aS)
                    P.wait("vector", aS)
                    P.do("vector", lambda e, pb=pb_off, c0=c0, lb=lb: e.tensor_scalar(
                        pbuf[:, pb:pb + c0], pbuf[:, pb:pb + c0], lrow[:, lb + 4:lb + 5], None, ALU.mult))
                    P.do("vector", lambda e, lb=lb: e.tensor_mul(lrow[:, lb + 1:lb + 2], lrow[:, lb + 1:lb + 2], lrow[:, lb + 4:lb + 5]), inc=vS)
                    P.wait("scalar", vS)
                    P.do("scalar", lambda e, reg=reg, bw=bw, pb=pb_off, c0=c0, lb=lb: e.activation(
                        pbuf[:, pb + c0: pb + c0 + bw], reg[:, 0:bw], AF.Exp,
                        bias=lrow[:, lb:lb + 1], accum_out=lrow[:, lb + 5:lb + 6]), inc=aS)
                    P.wait("vector", aS)
                    P.do("vector", lambda e, lb=lb: e.tensor_add(lrow[:, lb + 1:lb + 2], lrow[:, lb + 1:lb + 2], lrow[:, lb + 5:lb + 6]), inc=vS)
                exp_seen[rsel] = P.val(aS)
            P.wait("vector", aS)
            P.wait("vector", vS)
            P.do("vector", lambda e, lb=lb: e.reciprocal(lrow[:, lb + 6:lb + 7], lrow[:, lb + 1:lb + 2]), inc=vS)
            ustate[u] = (P.val(aS), P.val(vS))

        def part2(u):
            h, j = units[u]
            hb = h % 2
            lb = (u % 2) * 8
            pref = (j + 1) * 512
            pb_off = (u % 2) * 4096
            exp_a, recip_done = ustate.pop(u)
            nkt = pref // 128
            pvps = psE if u % 2 == 0 else psF
            for tb in range((nkt + 7) // 8):
                tpar = tb % 2
                tps = psT0 if tpar == 0 else psT1
                to = tpar * 1024
                n_in = min(8, nkt - tb * 8)
                P.wait("tensor", aS, exp_a)
                P.wait("tensor", vS, max(ptb_mark[tpar], otmark[tpar]))
                for t in range(n_in):
                    kt = tb * 8 + t
                    P.do("tensor", lambda e, kt=kt, t=t, tps=tps, pb=pb_off: e.transpose(
                        tps[:, t * 128:(t + 1) * 128], pbuf[:, pb + kt * 128: pb + (kt + 1) * 128], iden()),
                        inc=(tS if t == n_in - 1 else None))
                P.wait("vector", tS)
                P.do("vector", lambda e, tps=tps, to=to, n=n_in: e.tensor_copy(ptb[:, to:to + n * 128], tps[:, 0:n * 128]), inc=vS)
                ptb_mark[tpar] = P.val(vS)
                P.wait("tensor", vS, ptb_mark[tpar])
                if tb == 0:
                    P.wait("tensor", vS, oevac_mark.get(u - 2, 0))
                for t in range(n_in):
                    kt = tb * 8 + t
                    P.do("tensor", lambda e, kt=kt, t=t, to=to, hb=hb, pvps=pvps, nkt=nkt: e.matmul(
                        pvps[:, 0:128], ptb[:, to + t * 128: to + (t + 1) * 128],
                        vst[:, hb * 4096 + kt * 128: hb * 4096 + (kt + 1) * 128],
                        start=(kt == 0), stop=(kt == nkt - 1)), inc=(tS if kt == nkt - 1 else None))
            pv_done = P.val(tS)
            P.wait("vector", tS, pv_done)
            P.wait("vector", vS, recip_done)
            P.do("vector", lambda e, pvps=pvps, lb=lb: e.tensor_scalar(
                sqs[:, (u % 2) * 128:(u % 2) * 128 + 128], pvps[:, 0:128], lrow[:, lb + 6:lb + 7], None, ALU.mult), inc=vS)
            oevac_mark[u] = P.val(vS)
            P.wait("tensor", vS, oevac_mark[u])
            tps = psT0 if u % 2 == 0 else psT1
            P.do("tensor", lambda e, tps=tps: e.transpose(tps[:, 896:1024], sqs[:, (u % 2) * 128:(u % 2) * 128 + 128], iden()), inc=tS)
            P.wait("vector", tS)
            P.do("vector", lambda e, h=h, j=j, tps=tps: e.tensor_copy(
                xn[:, h * 1024 + j * 128: h * 1024 + (j + 1) * 128], tps[:, 896:1024]), inc=vS)
            otmark[u % 2] = P.val(vS)
            if j == NJ - 1:
                hmark[h] = P.val(tS)

        part1(0)
        for u in range(len(units)):
            if u + 1 < len(units):
                part1(u + 1)
            part2(u)
        P.wait("tensor", vS)
        P.wait("scalar", vS)

    def out_proj(w_ap, bias0):
        def evac(oc, ps):
            P.wait("vector", tS)
            for tt in range(2):
                dst = x_res[:, oc * 1024 + tt * 512: oc * 1024 + (tt + 1) * 512]
                if bias0 is None:
                    P.do("vector", lambda e, dst=dst, ps=ps, tt=tt: e.tensor_add(dst, ps[:, tt * 512:(tt + 1) * 512], dst), inc=vS)
                else:
                    P.do("vector", lambda e, dst=dst, ps=ps, tt=tt, oc=oc: e.scalar_tensor_tensor(
                        dst, ps[:, tt * 512:(tt + 1) * 512], small[:, bias0 + oc:bias0 + oc + 1], dst, ALU.add, ALU.add), inc=vS)
        proj_fm(w_ap, list(range(16)), evac)
        P.wait("vector", vS)

    QSHARDS = [("w1s", wq1s_d, 256, HID, SL_G1),
               ("w2s", wq2s_d, 256, HID, SL_G2),
               ("w3s", wq3s_d, 704, D, SL_G3)]

    def quant_pass1():
        P.wait("vector", dI)
        P.do("vector", lambda e: e.memset(lrow[0:1, 0:16], 0.0))
        P.do("vector", lambda e: e.memset(small[:, 83:84], 0.0), inc=vS)
        sidx = 0
        smarks = {}
        for mi, (wn, _, R, C, _slot) in enumerate(QSHARDS):
            w_ap = di[wn]
            ngrp = (R + 127) // 128
            nst = C // 512
            for gi in range(ngrp):
                rws = min(128, R - gi * 128)
                for stp in range(nst):
                    par = sidx % 2
                    qsem = dQ[par]
                    if sidx >= 2:
                        P.wait("gpsimd", vS, smarks[sidx - 2])
                    P.do("gpsimd", lambda e, w_ap=w_ap, gi=gi, stp=stp, rws=rws, par=par: e.dma_start(
                        out=scr[0:rws, par * 512:par * 512 + 512], in_=w_ap[gi * 128: gi * 128 + rws, stp * 512:(stp + 1) * 512]), inc=qsem, by=16)
                    P.wait("vector", qsem)
                    P.do("vector", lambda e, rws=rws, par=par: e.tensor_reduce(
                        small[0:rws, 84:85], scr[0:rws, par * 512:par * 512 + 512], AX, ALU.add, apply_absolute_value=True), inc=vS)
                    smarks[sidx] = P.val(vS)
                    P.wait("vector", vS)
                    P.do("vector", lambda e, rws=rws: e.tensor_add(
                        small[0:rws, 83:84], small[0:rws, 83:84], small[0:rws, 84:85]), inc=vS)
                    sidx += 1
            P.wait("tensor", vS)
            P.do("tensor", lambda e: e.matmul(psE[0:1, 0:1], constf[:, 0:1], small[:, 83:84], start=True, stop=True), inc=tS)
            P.wait("vector", tS)
            P.do("vector", lambda e, mi=mi: e.tensor_copy(lrow[0:1, 8 + mi: 9 + mi], psE[0:1, 0:1]))
            P.do("vector", lambda e: e.memset(small[:, 83:84], 0.0), inc=vS)
        P.wait("gpsimd", vS)
        P.do("gpsimd", lambda e: e.dma_start(out=gpart_d[0:1, 0:16], in_=lrow[0:1, 0:16]), inc=dG, by=16)
        P.do("gpsimd", lambda e: e.dma_start(out=dbgq_d[0:1, 0:16], in_=lrow[0:1, 0:16]), inc=dG, by=16)
        P.wait("gpsimd", dG)
        P.do("gpsimd", lambda e: e.collective_compute(
            "AllReduce", ALU.add, replica_groups=[[0, 1, 2, 3, 4, 5, 6, 7]],
            ins=[gpart_d.ap().opt()], outs=[gsum_d.ap().opt()]), inc=cc)
        return P.val(cc)

    def quant_pass2(ar_done):
        P.wait("gpsimd", cc, ar_done)
        P.wait("gpsimd", vS)
        P.do("gpsimd", lambda e: e.dma_start(out=lrow[0:1, 0:16], in_=gsum_d[0:1, 0:16]), inc=dG, by=16)
        gs_done = P.val(dG)
        P.wait("vector", dG, gs_done)
        for mi in range(3):
            P.do("vector", lambda e, mi=mi: e.tensor_scalar(
                lrow[0:1, 12 + mi:13 + mi], lrow[0:1, 8 + mi: 9 + mi], 1.0 / GN, 1e-8, ALU.mult, ALU.add), inc=vS)
            P.wait("vector", vS)
            P.do("vector", lambda e, mi=mi: e.reciprocal(lrow[0:1, 12 + mi:13 + mi], lrow[0:1, 12 + mi:13 + mi]), inc=vS)
            P.wait("vector", vS)
        P.wait("tensor", vS)
        for mi in range(3):
            P.do("tensor", lambda e, mi=mi: e.matmul(psE[:, mi:mi + 1], onesf_r(), lrow[0:1, 12 + mi:13 + mi], start=True, stop=True),
                 inc=(tS if mi == 2 else None))
        P.wait("vector", tS)
        for mi, (_, _, _, _, slot) in enumerate(QSHARDS):
            P.do("vector", lambda e, mi=mi, slot=slot: e.tensor_copy(small[:, slot:slot + 1], psE[:, mi:mi + 1]), inc=vS)
        sidx = 0
        smarks = {}
        stmark = {}
        for mi, (wn, wq_ap, R, C, slot) in enumerate(QSHARDS):
            w_ap = di[wn]
            ngrp = (R + 127) // 128
            nst = C // 512
            for gi in range(ngrp):
                rws = min(128, R - gi * 128)
                for stp in range(nst):
                    par = sidx % 2
                    qsem = dQ[par]
                    if sidx >= 2:
                        P.wait("gpsimd", vS, smarks[sidx - 2])
                    P.do("gpsimd", lambda e, w_ap=w_ap, gi=gi, stp=stp, rws=rws, par=par: e.dma_start(
                        out=scr[0:rws, par * 512:par * 512 + 512], in_=w_ap[gi * 128: gi * 128 + rws, stp * 512:(stp + 1) * 512]), inc=qsem, by=16)
                    P.wait("vector", qsem)
                    if sidx >= 2:
                        P.wait("vector", dS, stmark[sidx - 2])
                    P.do("vector", lambda e, rws=rws, slot=slot, par=par: e.tensor_scalar(
                        scr2[0:rws, par * 512:par * 512 + 512], scr[0:rws, par * 512:par * 512 + 512],
                        small[0:rws, slot:slot + 1], None, ALU.mult), inc=vS)
                    smarks[sidx] = P.val(vS)
                    P.wait("vector", vS)
                    P.do("vector", lambda e, rws=rws, par=par: e.tensor_single_scalar(
                        sqs[0:rws, par * 512:par * 512 + 512], scr2[0:rws, par * 512:par * 512 + 512], 0.5, ALU.is_ge))
                    P.do("vector", lambda e, rws=rws, par=par: e.tensor_single_scalar(
                        ptb[0:rws, par * 512:par * 512 + 512], scr2[0:rws, par * 512:par * 512 + 512], -0.5, ALU.is_le), inc=vS)
                    P.wait("vector", vS)
                    P.do("vector", lambda e, rws=rws, par=par: e.tensor_sub(
                        sqs[0:rws, par * 512:par * 512 + 512], sqs[0:rws, par * 512:par * 512 + 512],
                        ptb[0:rws, par * 512:par * 512 + 512]), inc=vS)
                    P.wait("sync", vS)
                    P.do("sync", lambda e, wq_ap=wq_ap, gi=gi, stp=stp, rws=rws, par=par: e.dma_start(
                        out=wq_ap.ap()[gi * 128: gi * 128 + rws, stp * 512:(stp + 1) * 512],
                        in_=sqs[0:rws, par * 512:par * 512 + 512]), inc=dS, by=16)
                    stmark[sidx] = P.val(dS)
                    sidx += 1
        st_done = P.val(dS)
        P.wait("gpsimd", dS, st_done)
        for s_d, g_d in [(wq1s_d, wq1_d), (wq2s_d, wq2_d), (wq3s_d, wq3_d)]:
            P.do("gpsimd", lambda e, s_d=s_d, g_d=g_d: e.collective_compute(
                "AllGather", ALU.bypass, replica_groups=[[0, 1, 2, 3, 4, 5, 6, 7]],
                ins=[s_d.ap().opt()], outs=[g_d.ap().opt()]), inc=cc)
        return P.val(cc)

    def hid_slice(s):
        return kst[:, s * 1024:(s + 1) * 1024] if s < 8 else vst[:, (s - 8) * 1024:(s - 7) * 1024]

    def ffn(wq_done, ln_done):
        P.wait("sync", cc, wq_done)
        st = Stream("sync", dP)
        for g in range(4):
            for li in range(11):
                hc = g * 11 + li
                for m in range(2):
                    w_d = wq1_d if m == 0 else wq2_d
                    def wload(e, buf, w_d=w_d, hc=hc):
                        return e.dma_start(
                            out=wsb[:, buf * 2048: buf * 2048 + 2048].rearrange("p (i c) -> p i c", c=128),
                            in_=w_d.ap()[:, hc * 128:(hc + 1) * 128].rearrange("(i p) c -> p i c", p=128))
                    buf, sem, rdy = st.load(wload)
                    P.wait("tensor", sem, rdy)
                    P.wait("tensor", aS if m == 0 else vS)
                    if g == 0 and li == 0 and m == 0:
                        P.wait("tensor", vS, ln_done)
                        P.wait("tensor", vS)
                    ps = psA if m == 0 else psB
                    for tt in range(2):
                        for i in range(NCH):
                            P.do("tensor", lambda e, buf=buf, i=i, tt=tt, ps=ps: e.matmul(
                                ps[:, tt * 512:(tt + 1) * 512],
                                wsb[:, buf * 2048 + i * 128: buf * 2048 + (i + 1) * 128],
                                xn[:, i * 1024 + tt * 512: i * 1024 + (tt + 1) * 512],
                                start=(i == 0), stop=(i == NCH - 1)), inc=(tS if i == NCH - 1 else None))
                    st.mark()
                    if m == 0:
                        P.wait("scalar", tS)
                        P.wait("scalar", vS)
                        P.do("scalar", lambda e: e.activation(sqs[:, 0:1024], psA[:, 0:1024], AF.Sigmoid), inc=aS)
                P.wait("vector", aS)
                P.wait("vector", tS)
                P.do("vector", lambda e, li=li: e.tensor_mul(hid_slice(li), sqs[:, 0:1024], psB[:, 0:1024]), inc=vS)
            for oc in range(16):
                def w3load(e, buf, g=g, oc=oc):
                    return e.dma_start(
                        out=pbuf[:, 4096 + buf * 1408: 4096 + buf * 1408 + 1408].rearrange("p (i c) -> p i c", c=128),
                        in_=wq3_d.ap()[g * 1408:(g + 1) * 1408, oc * 128:(oc + 1) * 128].rearrange("(i p) c -> p i c", p=128))
                buf, sem, rdy = st.load(w3load)
                P.wait("tensor", sem, rdy)
                P.wait("tensor", vS)
                for tt in range(2):
                    ps = psE if tt == 0 else psF
                    for i in range(11):
                        P.do("tensor", lambda e, buf=buf, i=i, tt=tt, ps=ps: e.matmul(
                            ps[:, 0:512],
                            pbuf[:, 4096 + buf * 1408 + i * 128: 4096 + buf * 1408 + (i + 1) * 128],
                            hid_slice(i)[:, tt * 512:(tt + 1) * 512],
                            start=(i == 0), stop=(i == 10)), inc=(tS if i == 10 else None))
                st.mark()
                P.wait("vector", tS)
                for tt in range(2):
                    ps = psE if tt == 0 else psF
                    dst = x_res[:, oc * 1024 + tt * 512: oc * 1024 + (tt + 1) * 512]
                    P.do("vector", lambda e, dst=dst, ps=ps: e.tensor_add(dst, ps[:, 0:512], dst), inc=vS)

    def tap(name):
        if name in TAPS:
            dbg = nc.dram_tensor("tap_" + name, [D, TOK], F32, kind="ExternalOutput").ap()
            P.wait("sync", vS)
            P.do("sync", lambda e, dbg=dbg: e.dma_start(
                out=dbg.rearrange("(i p) t -> p i t", p=128),
                in_=x_res[:].rearrange("p (i t) -> p i t", t=1024)), inc=dS, by=16)
            P.wait("vector", dS)

    P.phase = "loads"
    loads()
    if stage >= 2:
        P.phase = "quant1"
        ar_g = quant_pass1()
    P.phase = "ln1"
    ln1 = layernorm()
    P.phase = "prep_a"
    ag_a, sp_a = attn_a_prep(ln1)
    if stage >= 2:
        P.phase = "quant2"
        wq_done = quant_pass2(ar_g)
    P.phase = "attn_a"
    attention("a", ag_a, sp_a)
    P.phase = "oproj_a"
    out_proj(woT, None)
    tap("x2")
    if stage >= 2:
        P.phase = "ln2"
        ln2 = layernorm()
        P.phase = "ffn1"
        ffn(wq_done, ln2)
        tap("x3")
    if stage >= 3:
        P.phase = "ln3"
        ln3 = layernorm()
        P.phase = "prep_b"
        ag_b, sp_b = attn_b_prep(ln3)
        P.phase = "attn_b"
        attention("b", ag_b, sp_b)
        P.phase = "oproj_b"
        out_proj(wout, SL_OB0)
        tap("x4")
    if stage >= 4:
        P.phase = "ln4"
        ln4 = layernorm()
        P.phase = "ffn2"
        ffn(wq_done, ln4)
    P.phase = "store"
    P.wait("sync", vS)
    P.do("sync", lambda e: e.dma_start(
        out=out_d.rearrange("(i p) t -> p i t", p=128),
        in_=x_res[:].rearrange("p (i t) -> p i t", t=1024)), inc=dS, by=16)
    if DUMP:
        interns = {t.name: t for t in [kA_loc, vA_loc, kA_g, vA_g, kB_loc, vB_loc, kB_g, vB_g,
                                       q_sp, qR_sp, wq1s_d, wq2s_d, wq3s_d, wq1_d, wq2_d, wq3_d,
                                       gpart_d, gsum_d, dbgq_d]}
        for nm in DUMP:
            t = interns[nm]
            dbg = nc.dram_tensor("dbg_" + nm, list(t.shape), t.dtype, kind="ExternalOutput").ap()
            P.wait("gpsimd", cc)
            P.wait("gpsimd", vS)
            P.do("gpsimd", lambda e, dbg=dbg, t=t: e.dma_start(out=dbg, in_=t.ap()), inc=dG, by=16)
        P.wait("gpsimd", dG)
    P.wait("sync", dS)

    import contextlib
    es = contextlib.ExitStack()
    for name in P.cnt:
        P.handles[name] = es.enter_context(nc.semaphore(name))
    def replay(eng, queue):
        cur = None
        import contextlib as _cl
        stack = _cl.ExitStack()
        for f in queue:
            ph = getattr(f, "phase", None)
            if ph is not None and ph != cur:
                stack.close()
                stack = _cl.ExitStack()
                stack.enter_context(nc.named_scope(ph))
                cur = ph
            f(eng)
        stack.close()

    with es, nc.Block() as block:
        @block.tensor
        def _(eng):
            replay(eng, P.q["tensor"])

        @block.vector
        def _(eng):
            replay(eng, P.q["vector"])

        @block.scalar
        def _(eng):
            replay(eng, P.q["scalar"])

        @block.gpsimd
        def _(eng):
            replay(eng, P.q["gpsimd"])

        @block.sync
        def _(eng):
            replay(eng, P.q["sync"])
    return nc



def _host_constants():
    c = {}
    ones = np.ones((128, 128), np.float32)
    c["ones_bf"] = ones.astype(BF)
    c["ones_f"] = ones.astype(np.float32)
    c["iden_bf"] = np.eye(128, dtype=np.float32).astype(BF)
    rA = np.zeros((128, 128), np.float32)
    for i in range(64):
        rA[i, i + 64] = -1.0
        rA[i + 64, i] = 1.0
    c["rotA"] = rA.T.copy().astype(BF)
    rB = np.zeros((128, 128), np.float32)
    for i in range(64):
        rB[2 * i, 2 * i + 1] = -1.0
        rB[2 * i + 1, 2 * i] = 1.0
    c["rotB"] = rB.T.copy().astype(BF)
    return c


def _pos_own(cid):
    c = cid % 4
    j = np.arange(NJ)
    g = 4 * j + c
    return (g[:, None] * 128 + np.arange(128)[None, :]).reshape(-1).astype(np.float64)


def _tables(cid):
    t = {}
    pos = _pos_own(cid)
    inv = 1.0 / (1e6 ** (np.arange(0, 128, 2) / 128.0))
    fr = pos[None, :] * np.concatenate([inv, inv])[:, None]
    t["cosA"] = np.cos(fr).astype(np.float32)
    t["sinA"] = np.sin(fr).astype(np.float32)
    thk = np.exp(-2.0 * np.repeat(np.arange(DHR // 2), 2) / DHR * np.log(10000.0))
    frk = (pos[None, :] + 1.0) * thk[:, None]
    ck = np.zeros((128, TOK), np.float32)
    sk = np.zeros((128, TOK), np.float32)
    ck[:64] = np.cos(frk)
    sk[:64] = np.sin(frk)
    t["cosKR"] = ck
    t["sinKR"] = sk
    thq = np.exp(-2.0 * np.repeat(np.arange(512), 2) / 1024.0 * np.log(10000.0))
    cq = np.empty((128, 8 * TOK), np.float32)
    sq = np.empty((128, 8 * TOK), np.float32)
    for ch in range(8):
        frq = (pos[None, :] + 1.0) * thq[ch * 128:(ch + 1) * 128][:, None]
        cq[:, ch * TOK:(ch + 1) * TOK] = np.cos(frq)
        sq[:, ch * TOK:(ch + 1) * TOK] = np.sin(frq)
    t["cosQR"] = cq
    t["sinQR"] = sq
    m = np.zeros((128, 512), np.float32)
    c = cid % 4
    pp = c * 128 + np.arange(128)
    m[pp[:, None] < np.arange(512)[None, :]] = NEG
    t["maskD"] = m
    return t


def _bias_fm(b, nch):
    return np.ascontiguousarray(np.asarray(b).reshape(nch, 128).T).astype(np.float32)


def make_in_maps(inputs):
    g = {k: np.asarray(v) for k, v in inputs.items()}
    x = g["L0_attn_a_x"]
    shared = {
        "wqkvT": np.ascontiguousarray(g["L0_attn_a_w_qkv"].T).astype(np.float32),
        "woT": np.ascontiguousarray(g["L0_attn_a_w_o"].T).astype(np.float32),
        "qn_g": g["L0_attn_a_qn_g"].reshape(128, 1).astype(np.float32),
        "qn_b": g["L0_attn_a_qn_b"].reshape(128, 1).astype(np.float32),
        "kn_g": g["L0_attn_a_kn_g"].reshape(128, 1).astype(np.float32),
        "kn_b": g["L0_attn_a_kn_b"].reshape(128, 1).astype(np.float32),
        "wdown": np.ascontiguousarray(g["L0_attn_b_W_down_k"]).astype(np.float32),
        "wdown_b": _bias_fm(g["L0_attn_b_W_down_b"], 8),
        "wukv": np.ascontiguousarray(g["L0_attn_b_W_uKV_k"]).astype(np.float32),
        "wukv_bk": _bias_fm(g["L0_attn_b_W_uKV_b"][:D], 16),
        "wukv_bv": g["L0_attn_b_W_uKV_b"][D:].reshape(1, D).astype(np.float32),
        "wuq": np.ascontiguousarray(g["L0_attn_b_W_uQ_k"]).astype(np.float32),
        "wuq_b": _bias_fm(g["L0_attn_b_W_uQ_b"], 16),
        "wkr": np.ascontiguousarray(g["L0_attn_b_Wkr_k"]).astype(np.float32),
        "wkr_b": np.pad(np.asarray(g["L0_attn_b_Wkr_b"]), (0, 64)).reshape(1, 128).T.copy().astype(np.float32),
        "wqr": np.ascontiguousarray(g["L0_attn_b_Wqr_k"]).astype(np.float32),
        "wqr_b": _bias_fm(g["L0_attn_b_Wqr_b"], 8),
        "wout": np.ascontiguousarray(g["L0_attn_b_Wout_k"]).astype(np.float32),
        "wout_b": _bias_fm(g["L0_attn_b_Wout_b"], 16),
    }
    w1T = np.ascontiguousarray(g["L0_ffn_w1"].T).astype(np.float32)
    w2T = np.ascontiguousarray(g["L0_ffn_w2"].T).astype(np.float32)
    w3T = np.ascontiguousarray(g["L0_ffn_w3"].T).astype(np.float32)
    consts = _host_constants()

    in_maps = []
    for cid in range(8):
        b, c = cid // 4, cid % 4
        jj = 4 * np.arange(NJ) + c
        xs = x[b].reshape(32, 128, D)[jj]
        m = dict(shared)
        m.update(consts)
        m.update(_tables(cid))
        m["x_fm"] = np.ascontiguousarray(xs.reshape(TOK, D).T).astype(np.float32)
        m["w1s"] = np.ascontiguousarray(w1T[cid * 256:(cid + 1) * 256]).astype(np.float32)
        m["w2s"] = np.ascontiguousarray(w2T[cid * 256:(cid + 1) * 256]).astype(np.float32)
        m["w3s"] = np.ascontiguousarray(w3T[cid * 704:(cid + 1) * 704]).astype(np.float32)
        in_maps.append(m)
    return in_maps


def assemble(results):
    out = np.empty((B, T, D), np.float32)
    for cid in range(8):
        b, c = cid // 4, cid % 4
        oc = np.asarray(results[cid]["out"])
        jj = 4 * np.arange(NJ) + c
        out[b].reshape(32, 128, D)[jj] = oc.T.reshape(NJ, 128, D)
    return out


def kernel(**inputs):
    in_maps = make_in_maps(inputs)
    nc = bass.Bass("TRN2", target_bir_lowering=False, detect_race_conditions=False)
    build(nc, STAGE)
    res = run_bass_kernel_spmd(nc, in_maps, core_ids=list(range(8)))
    return assemble(res.results)
